# version 21
# baseline (speedup 1.0000x reference)
import numpy as np
import sys
from contextlib import ExitStack

sys.path.insert(0, "/opt/trn_rl_repo")
import concourse.bass as bass
import concourse.mybir as mybir
from concourse.bass_utils import run_bass_kernel_spmd

F32 = mybir.dt.float32
BF16 = mybir.dt.bfloat16
AF = mybir.ActivationFunctionType
ALU = mybir.AluOpType
AXX = mybir.AxisListType.X

T = 4096
DM = 1024
NCH = 8
D_INNER = 2048
NH = 32
HP = 64
NXBC = 18
NMCH = 35
MEGA = 512
NMEGA = T // MEGA
EPS = 1e-5
SHIFT = 50.0

DBG_PHASES = "all"
DBG_DUMP = False


def fbc(ap, n=128):
    return bass.AP(ap.tensor, ap.offset, [list(ap.ap[0]), [0, n]])


def v3(ap, rep, n=128):
    return bass.AP(ap.tensor, ap.offset, [list(ap.ap[0]), [n, rep], [1, n]])


def rep_mid(ap, rep, n=128):
    return bass.AP(ap.tensor, ap.offset, [list(ap.ap[0]), [0, rep], [1, n]])


def colrep(ap, col0, nh, rep=64):
    return bass.AP(ap.tensor, ap.offset + col0,
                   [list(ap.ap[0]), [1, nh], [0, rep]])


def kchunk_view(ap2, nk=8, nm=128):
    return bass.AP(ap2.tensor, ap2.offset, [[nm, 128], [128 * nm, nk], [1, nm]])


def tile_view(dram_ap, tlo, nt, nch=NCH):
    return bass.AP(dram_ap.tensor, dram_ap.offset + tlo,
                   [[T, 128], [128 * T, nch], [1, nt]])


class Sched:
    ENGINES = ("tensor", "vector", "scalar", "gpsimd", "sync")

    def __init__(self, nc):
        self.nc = nc
        self.q = {e: [] for e in self.ENGINES}
        self.sems = {}
        self.cnt = {}
        self.last = {e: None for e in self.ENGINES}
        self.esem = {e: self.mksem("ms_" + e) for e in self.ENGINES}
        self.qch = set()
        self.dirty = {e: set() for e in self.ENGINES}
        self.rdset = {e: set() for e in self.ENGINES}

    def mksem(self, name):
        h = self.nc.semaphore(name).__enter__()
        self.sems[name] = h
        self.cnt[name] = 0
        return name

    def wait(self, eng, sem, val):
        if val > 0:
            self.q[eng].append(["wait", sem, val])

    def waits(self, eng, ws):
        seen = {}
        for w in ws:
            if w is None:
                continue
            s, v = w
            seen[s] = max(seen.get(s, 0), v)
        for s, v in seen.items():
            self.wait(eng, s, v)

    @staticmethod
    def _ap_names(args, kwargs):
        names = []
        for a in args:
            if isinstance(a, bass.AP):
                names.append(a.tensor.name)
        for a in kwargs.values():
            if isinstance(a, bass.AP):
                names.append(a.tensor.name)
        return names

    def op(self, eng, method, *args, waits=(), **kwargs):
        if eng == "tensor":
            self.waits(eng, waits)
            ent = ["ins", method, args, kwargs, []]
            self.q[eng].append(ent)
            self.last[eng] = ent
            return ent
        names = self._ap_names(args, kwargs)
        out_n = names[0] if names else None
        reads = set(names[1:])
        need = (bool(reads & self.dirty[eng])
                or (out_n in self.dirty[eng])
                or (out_n in self.rdset[eng]))
        if need and self.last[eng] is not None:
            w = None
            ent = self.last[eng]
            for (sem, by) in ent[4]:
                if sem == self.esem[eng]:
                    w = (sem, self.cnt[sem])
            if w is None:
                if ent[1] == "dma_start":
                    w = None
                else:
                    w = self.ms(eng)
            if w is not None:
                self.wait(eng, w[0], w[1])
            self.dirty[eng].clear()
            self.rdset[eng].clear()
        self.waits(eng, waits)
        ent = ["ins", method, args, kwargs, []]
        self.q[eng].append(ent)
        self.last[eng] = ent
        if out_n:
            self.dirty[eng].add(out_n)
        self.rdset[eng] |= reads
        return ent

    def ms(self, eng, sem=None, by=1):
        sem = sem or self.esem[eng]
        ent = self.last[eng]
        assert ent is not None
        self.cnt[sem] += by
        ent[4].append((sem, by))
        return (sem, self.cnt[sem])

    def now(self, eng):
        return (self.esem[eng], self.cnt[self.esem[eng]])

    def chain(self, eng):
        w = self.ms(eng)
        self.wait(eng, w[0], w[1])
        return w

    def dma(self, queue, dst, src, waits=(), ch="misc"):
        sem = "q_" + ch
        if sem not in self.sems:
            self.mksem(sem)
            self.qch.add(sem)
        self.waits(queue, waits)
        ent = ["ins", "dma_start", (dst, src), {}, []]
        self.q[queue].append(ent)
        self.last[queue] = ent
        self.cnt[sem] += 16
        ent[4].append((sem, 16))
        return (sem, self.cnt[sem])

    def barrier(self):
        vals = {}
        for e in self.ENGINES:
            ent = self.last[e]
            if ent is not None and ent[1] != "dma_start" and not ent[4]:
                vals[e] = self.ms(e)
            else:
                vals[e] = self.now(e)
        qvals = [(sem, self.cnt[sem]) for sem in sorted(self.qch)]
        for e in self.ENGINES:
            for o in self.ENGINES:
                if o != e:
                    self.wait(e, vals[o][0], vals[o][1])
            for sem, v in qvals:
                self.wait(e, sem, v)

    def replay(self, eng, h):
        for ent in self.q[eng]:
            if ent[0] == "wait":
                h.wait_ge(self.sems[ent[1]], ent[2])
            else:
                ins = getattr(h, ent[1])(*ent[2], **ent[3])
                for (sem, by) in ent[4]:
                    ins.then_inc(self.sems[sem], by)


class Ctx:
    pass


def bgw(c, *names):
    return [c.bg[n] for n in names if n in c.bg]


def bgs(c, name, w):
    c.bg[name] = w


def build_core_program():
    nc = bass.Bass("TRN2", target_bir_lowering=False, debug=False)
    c = Ctx()
    c.nc = nc
    c.bg = {}

    def din(name, shape):
        return nc.dram_tensor(name, shape, F32, kind="ExternalInput").ap()

    c.x_in = din("x", [T, DM])
    c.win_l = din("win_l", [NMCH, DM, 128])
    c.wout = din("wout", [D_INNER, DM])
    c.conv_w = din("conv_w", [128, NXBC * 4])
    c.conv_b = din("conv_b", [128, NXBC])
    c.norm_w = din("norm_w", [128, 16])
    c.d_rep = din("d_rep", [128, 16])
    c.dt_bias = din("dt_bias", [32, 1])
    c.a_log = din("a_log", [32, 1])
    c.wqkv_l = din("wqkv_l", [24, DM, 128])
    c.b_attn = din("b_attn", [128, 24])
    c.wproj = din("wproj", [DM, DM])
    c.b_proj = din("b_proj", [128, NCH])
    c.wrT = din("wrT", [DM, DM])
    c.wkT = din("wkT", [DM, DM])
    c.wvT = din("wvT", [DM, DM])
    c.mu_r = din("mu_r", [128, NCH])
    c.mu_k = din("mu_k", [128, NCH])
    c.c_id = din("c_id", [128, 128])
    c.c_tril = din("c_tril", [128, 128])
    c.c_trius = din("c_trius", [128, 128])
    c.c_mask01 = din("c_mask01", [128, 128])
    c.c_cos = din("c_cos", [128, T])
    c.c_sin = din("c_sin", [128, T])
    c.c_perm = din("c_perm", [128, 128])
    c.c_ones = din("c_ones", [128, 128])
    c.c_bias = din("c_bias", [128, 2])
    c.c_onec = din("c_onec", [128, 1])
    c.battn_r = din("battn_r", [1, 24 * 128])
    c.bproj_r = din("bproj_r", [1, NCH * 128])

    c.y_out = nc.dram_tensor("y", [T, DM], F32, kind="ExternalOutput").ap()
    c.xres = nc.dram_tensor("xres", [NCH, 128, T], F32, kind="Internal").ap()
    c.xn_h = nc.dram_tensor("xn_h", [NCH, 128, T], BF16, kind="Internal").ap()
    c.winb_h = nc.dram_tensor("winb_h", [NMCH, 128, NCH * 128], BF16, kind="Internal").ap()
    c.q_hbm = nc.dram_tensor("q_hbm", [NCH, 128, T], BF16, kind="Internal").ap()
    c.den_h = nc.dram_tensor("den_h", [16, MEGA], F32, kind="Internal").ap()
    c.stats_h = nc.dram_tensor("stats_h", [4, T], F32, kind="Internal").ap()
    c.dta_h = nc.dram_tensor("dta_h", [2, 32, T], F32, kind="Internal").ap()
    c.denb_h = nc.dram_tensor("denb_h", [1, 16 * MEGA], BF16, kind="Internal").ap()

    S = Sched(nc)
    c.S = S

    al = nc.alloc_sbuf_tensor
    for nm, sh, dt in [
            ("idf", [128, 128], F32), ("idb", [128, 128], BF16),
            ("tril", [128, 128], F32), ("trius", [128, 128], F32),
            ("mask01b", [128, 128], BF16),
            ("onescf", [128, 1], F32), ("onescb", [128, 1], BF16),
            ("onesrf", [1, 128], F32), ("onesrb", [1, 128], BF16),
            ("convw_sb", [128, NXBC * 4], F32), ("convb_sb", [128, NXBC], F32),
            ("normw_sb", [128, 16], F32), ("drep_sb", [128, 16], F32),
            ("dtb_sb", [32, 1], F32), ("alog_sb", [32, 1], F32),
            ("A_sb", [32, 1], F32),
            ("mur_sb", [128, NCH], F32), ("muk_sb", [128, NCH], F32),
            ("battn_sb", [128, 24], F32), ("bproj_sb", [128, NCH], F32),
            ("ebias", [128, 2], F32),
            ("r1", [1, 512], F32), ("r2", [1, 512], F32), ("r3", [1, 512], F32),
            ]:
        setattr(c, nm, al(nm, sh, dt).ap())

    c.pbank = {}
    for nm in ["A", "B"]:
        c.pbank[nm] = nc.alloc_psum_tensor("p" + nm, [128, 512], F32).ap()
    for pnm in ["CD", "EF"]:
        pp = nc.alloc_psum_tensor("p" + pnm, [128, 1024], F32).ap()
        c.pbank[pnm] = pp
        c.pbank[pnm[0]] = pp[:, 0:512]
        c.pbank[pnm[1]] = pp[:, 512:1024]
    c.pbank["T1"] = nc.alloc_psum_tensor("pT1", [128, 1024], BF16).ap()
    c.pbank["T2"] = nc.alloc_psum_tensor("pT2", [128, 1024], BF16).ap()

    phase0(c)
    S.barrier()
    c.bg = {}
    if DBG_DUMP:
        dxn = nc.dram_tensor("dbg_xn", [NCH, 128, T], BF16, kind="ExternalOutput").ap()
        ddta = nc.dram_tensor("dbg_dta", [2, 32, T], F32, kind="ExternalOutput").ap()
        dxr = nc.dram_tensor("dbg_xres0", [NCH, 128, T], F32, kind="ExternalOutput").ap()
        S.dma("sync", dxn, c.xn_h, ch="dbg")
        S.dma("sync", ddta, c.dta_h, ch="dbg")
        S.dma("sync", dxr, c.xres, ch="dbg")
        S.barrier()
    phaseA(c)
    S.barrier()
    c.bg = {}
    if DBG_PHASES == "A":
        emit_out(c, src="xres")
    else:
        phaseB(c, tag=1)
        S.barrier()
        c.bg = {}
        if DBG_PHASES == "B1":
            emit_out(c, src="xres")
        else:
            phaseC(c)
            S.barrier()
            c.bg = {}
            if DBG_PHASES == "C":
                emit_out(c, src="xres")
            else:
                phaseB(c, tag=2)

    for sem in sorted(S.qch):
        S.wait("sync", sem, S.cnt[sem])
        S.wait("gpsimd", sem, S.cnt[sem])

    with nc.Block() as block:
        @block.tensor
        def _(e):
            S.replay("tensor", e)

        @block.vector
        def _(e):
            S.replay("vector", e)

        @block.scalar
        def _(e):
            S.replay("scalar", e)

        @block.gpsimd
        def _(e):
            S.replay("gpsimd", e)

        @block.sync
        def _(e):
            S.replay("sync", e)

    return nc


def phase0(c):
    S, nc = c.S, c.nc
    pb = c.pbank
    last = None
    for dst, src in [(c.idf, c.c_id), (c.tril, c.c_tril), (c.trius, c.c_trius),
                     (c.onescf, c.c_onec), (c.onesrf, c.c_ones[0:1, :]),
                     (c.convw_sb, c.conv_w), (c.convb_sb, c.conv_b),
                     (c.normw_sb, c.norm_w), (c.drep_sb, c.d_rep),
                     (c.dtb_sb, c.dt_bias), (c.alog_sb, c.a_log),
                     (c.mur_sb, c.mu_r), (c.muk_sb, c.mu_k),
                     (c.battn_sb, c.b_attn), (c.bproj_sb, c.b_proj),
                     (c.ebias, c.c_bias)]:
        w_c = S.dma("sync", dst, src, ch="c0")
    for dst, src in [(c.idb, c.c_id), (c.mask01b, c.c_mask01),
                     (c.onescb, c.c_onec), (c.onesrb, c.c_ones[0:1, :])]:
        w_g = S.dma("gpsimd", dst, src, ch="g0")

    for e in Sched.ENGINES:
        S.wait(e, w_c[0], w_c[1])
        S.wait(e, w_g[0], w_g[1])
    S.op("scalar", "activation", c.A_sb, c.alog_sb, AF.Exp, waits=[w_c])
    w_sA = S.ms("scalar")
    S.op("vector", "tensor_scalar", c.A_sb, c.A_sb, -1.0, None, ALU.mult, waits=[w_sA])
    c.w_A = S.ms("vector")

    c.stW = ExitStack()
    c.Sst = c.stW.enter_context(nc.sbuf_tensor("Sst", [128, NH * HP], F32)).ap()
    c.Sstb = c.stW.enter_context(nc.sbuf_tensor("Sstb", [128, NH * HP], BF16)).ap()
    c.wdt_sb = c.stW.enter_context(nc.sbuf_tensor("wdt_sb", [128, NCH * 128], BF16)).ap()
    w_wdt = S.dma("gpsimd", c.wdt_sb, kchunk_view(c.win_l[34]), ch="gw")
    st = {}
    w_cstm = {}
    with nc.sbuf_tensor("wtmp", [128, 2 * NCH * 128], BF16) as wtg, \
         nc.sbuf_tensor("wtmpf", [128, 2 * NCH * 128], F32) as wtfg:
        wtmp = wtg.ap()
        wtmpf = wtfg.ap()
        for m in range(NMCH):
            b = m % 2
            sl = wtmp[:, b * 1024:(b + 1) * 1024]
            slf = wtmpf[:, b * 1024:(b + 1) * 1024]
            src = kchunk_view(c.win_l[m])
            w = [w_cstm[m - 2]] if m >= 2 else []
            ldw = S.dma("sync", slf, src, waits=w, ch="wr%d" % b)
            S.op("scalar", "copy", sl, slf,
                 waits=[ldw] + ([st[m - 2]] if m >= 2 else []))
            w_cstm[m] = S.ms("scalar")
            st[m] = S.dma("sync", c.winb_h[m], sl, waits=[w_cstm[m]], ch="ws%d" % b)
    c.woutb = c.stW.enter_context(nc.sbuf_tensor("woutb", [128, 16 * 1024], BF16)).ap()
    for kc in range(16):
        w = [st[NMCH - 1], st[NMCH - 2]] if kc < 2 else []
        S.dma("gpsimd", c.woutb[:, kc * 1024:(kc + 1) * 1024],
              c.wout[kc * 128:(kc + 1) * 128, :], ch="g%d" % (kc % 2), waits=w)

    st0 = ExitStack()
    dt_sb = st0.enter_context(nc.sbuf_tensor("dt_sb", [32, T], F32)).ap()
    dtr_sb = st0.enter_context(nc.sbuf_tensor("dtr_sb", [32, T], F32)).ap()
    a_sb = st0.enter_context(nc.sbuf_tensor("a_sb", [32, T], F32)).ap()
    xtm = st0.enter_context(nc.sbuf_tensor("xtm", [128, 4 * DM], F32)).ap()
    x2t = st0.enter_context(nc.sbuf_tensor("x2t", [128, DM], F32)).ap()
    xnt = st0.enter_context(nc.sbuf_tensor("xnt0", [128, 4 * DM], BF16)).ap()
    xcs = st0.enter_context(nc.sbuf_tensor("xcs", [128, 4 * DM], F32)).ap()
    xns = st0.enter_context(nc.sbuf_tensor("xns", [128, 4 * DM], BF16)).ap()
    statb = st0.enter_context(nc.sbuf_tensor("stat", [128, 32], F32)).ap()

    w_xstore = {}
    w_nstore = {}
    w_tdone = {}
    w_xnapply = {}
    w_dtev = {}
    w_xnsdone = {}
    for j in range(32):
        b = j % 4
        xt = xtm[:, b * DM:(b + 1) * DM]
        xn_t = xnt[:, b * DM:(b + 1) * DM]
        xc_s = xcs[:, b * DM:(b + 1) * DM]
        xn_s = xns[:, b * DM:(b + 1) * DM]
        stat = statb[:, b * 8:(b + 1) * 8]
        w = []
        if j >= 4:
            w = [w_tdone[j - 4], w_xnapply[j - 4]]
        ld = S.dma("sync", xt, c.x_in[j * 128:(j + 1) * 128, :], waits=w, ch="xt%d" % b)
        S.op("vector", "reduce_sum", stat[:, 0:1], xt, axis=AXX, waits=[ld])
        S.op("vector", "tensor_mul", x2t, xt, xt)
        S.op("vector", "reduce_sum", stat[:, 1:2], x2t, axis=AXX)
        S.op("vector", "tensor_scalar", stat[:, 2:3], stat[:, 0:1], 1.0 / DM, None, ALU.mult)
        S.chain("vector")
        S.op("vector", "tensor_mul", stat[:, 3:4], stat[:, 2:3], stat[:, 2:3])
        S.chain("vector")
        S.op("vector", "scalar_tensor_tensor", stat[:, 3:4], stat[:, 1:2], 1.0 / DM,
             stat[:, 3:4], ALU.mult, ALU.subtract)
        w_var = S.ms("vector")
        S.op("scalar", "activation", stat[:, 4:5], stat[:, 3:4], AF.Ln,
             bias=c.ebias[:, 0:1], scale=1.0, waits=[w_var])
        S.chain("scalar")
        S.op("scalar", "activation", stat[:, 4:5], stat[:, 4:5], AF.Exp,
             bias=0.0, scale=-0.5)
        w_rstd = S.ms("scalar")
        S.op("vector", "tensor_scalar", xn_t, xt, stat[:, 2:3], stat[:, 4:5],
             ALU.subtract, ALU.mult, waits=[w_rstd])
        w_xnapply[j] = S.ms("vector")
        bka, bkb = ("A", "B") if j % 2 == 0 else ("E", "F")
        bkt1 = "T1" if j % 2 == 0 else "T2"
        bkc = "C" if j % 2 == 0 else "D"
        for cc in range(NCH):
            bk = bka if cc < 4 else bkb
            w = (bgw(c, bk) + [ld]) if cc % 4 == 0 else []
            S.op("tensor", "transpose", pb[bk][:, (cc % 4) * 128:(cc % 4) * 128 + 128],
                 xt[:, cc * 128:(cc + 1) * 128], c.idf, waits=w)
            if cc % 4 == 3:
                w_tp = S.ms("tensor")
                w2 = [w_tp] + ([w_xstore[j - 4]] if (j >= 4 and cc == 3) else [])
                S.op("scalar", "copy", xc_s[:, (cc - 3) * 128:(cc + 1) * 128],
                     pb[bk][:, 0:512], waits=w2)
                bgs(c, bk, S.ms("scalar"))
        w_xcs = S.now("scalar")
        for cc in range(NCH):
            w = bgw(c, bkt1) + [w_xnapply[j]] if cc == 0 else []
            if j == 0 and cc == 0:
                w.append(w_g)
            S.op("tensor", "transpose", pb[bkt1][:, cc * 128:(cc + 1) * 128],
                 xn_t[:, cc * 128:(cc + 1) * 128], c.idb, waits=w)
        w_tp = S.ms("tensor")
        w2 = [w_tp]
        if j >= 4:
            w2 += [w_nstore[j - 4], w_xnsdone[j - 4]]
        S.op("vector", "tensor_copy", xn_s, pb[bkt1][:, 0:1024], waits=w2)
        bgs(c, bkt1, S.ms("vector"))
        w_xns = S.now("vector")
        w = [w_wdt, w_xns] + bgw(c, bkc)
        for kc in range(NCH):
            S.op("tensor", "matmul", pb[bkc][0:32, 0:128],
                 c.wdt_sb[:, kc * 128:kc * 128 + 32], xn_s[:, kc * 128:(kc + 1) * 128],
                 start=(kc == 0), stop=(kc == 7), waits=w if kc == 0 else [])
        w_mm = S.ms("tensor")
        w_tdone[j] = S.now("tensor")
        w_xnsdone[j] = w_mm
        S.op("vector", "tensor_copy", dtr_sb[:, j * 128:(j + 1) * 128],
             pb[bkc][0:32, 0:128], waits=[w_mm])
        bgs(c, bkc, S.ms("vector"))
        w_xstore[j] = S.dma("sync",
                            c.xres.rearrange("c p t -> p c t")[:, :, j * 128:(j + 1) * 128],
                            xc_s, waits=[w_xcs], ch="xs%d" % b)
        w_nstore[j] = S.dma("sync",
                            c.xn_h.rearrange("c p t -> p c t")[:, :, j * 128:(j + 1) * 128],
                            xn_s, waits=[w_xns], ch="ns%d" % b)

    S.op("vector", "tensor_scalar", dt_sb, dtr_sb, c.dtb_sb[:, 0:1], None,
         ALU.add)
    w_dtb = S.chain("vector")
    S.op("scalar", "activation", dtr_sb, dt_sb, AF.Abs, waits=[w_dtb])
    S.op("scalar", "activation", dtr_sb, dtr_sb, AF.Exp, bias=0.0, scale=-1.0)
    S.op("scalar", "activation", dtr_sb, dtr_sb, AF.Ln, bias=1.0, scale=1.0)
    w_sp = S.ms("scalar")
    S.op("vector", "tensor_scalar", a_sb, dt_sb, 0.0, None, ALU.max)
    S.chain("vector")
    S.op("vector", "tensor_add", dt_sb, dtr_sb, a_sb, waits=[w_sp])
    S.chain("vector")
    S.op("vector", "tensor_scalar", a_sb, dt_sb, c.A_sb[:, 0:1], None, ALU.mult,
         waits=[c.w_A])
    w_dta = S.ms("vector")
    S.dma("sync", c.dta_h[0], dt_sb, waits=[w_dta], ch="dta")
    S.dma("sync", c.dta_h[1], a_sb, waits=[w_dta], ch="dta")
    S.op("vector", "memset", c.Sst, 0.0)
    S.op("vector", "memset", c.Sstb, 0.0)
    S.ms("vector")
    st0.close()


def phaseA(c):
    S, nc = c.S, c.nc
    pb = c.pbank
    stA = ExitStack()

    def al(nm, sh, dt):
        return stA.enter_context(nc.sbuf_tensor(nm, sh, dt))

    winm = al("winm", [128, 3 * NCH * 128], BF16).ap()
    xnb = al("xnb", [128, 2 * NCH * 512], BF16).ap()
    sz = al("sz", [128, 16 * 512], BF16).ap()
    xbc = al("xbc", [128, NXBC * 515], BF16).ap()
    xbca = bass.AP(xbc.tensor, xbc.offset + 3,
                   [list(xbc.ap[0]), [515, NXBC], [1, 512]])
    y2 = al("y2", [128, 16 * 512], BF16).ap()
    xrt = al("xrt", [128, NCH * 512], F32).ap()
    dw = al("dw", [128, 4 * 512], BF16).ap()
    t8w = al("t8w", [128, 1024], BF16).ap()
    Uall = al("Uall", [128, NH * HP], BF16).ap()
    Xtall = al("Xtall", [128, NH * HP], BF16).ap()
    aT = al("aT", [128, 128], F32).ap()
    cumcT = al("cumcT", [128, 128], F32).ap()
    revT = al("revT", [128, 128], F32).ap()
    EcumT = al("EcumT", [128, 128], F32).ap()
    sc2T = al("sc2T", [128, 128], F32).ap()
    dtT = al("dtT", [128, 128], F32).ap()
    AtotB = al("AtotB", [128, 128], F32).ap()
    exl4 = al("exl4", [1, 128], F32).ap()
    Gmsk = al("Gmsk", [128, 128], BF16).ap()
    BT = al("BT", [128, 128], BF16).ap()
    D4 = al("D4", [128, 1024], F32).ap()
    E4 = al("E4", [128, 1024], F32).ap()
    Gm4 = al("Gm4", [128, 2 * 1024], BF16).ap()
    tYb = al("tYb", [128, NH * HP], BF16).ap()
    ypra = al("ypra", [128, NH * HP], BF16).ap()
    g2t = al("g2t", [128, 2 * 512], BF16).ap()
    tmpS = al("tmpS", [128, 512], F32).ap()
    rsr = al("rsr", [1, 512], F32).ap()
    halo = al("halo", [128, NXBC * 3], BF16).ap()
    dtm = al("dtm", [32, 512], F32).ap()
    am = al("am", [32, 512], F32).ap()

    S.op("vector", "memset", halo, 0.0)
    w_init = S.ms("vector")

    def xav(cc, qsl2=None):
        off = xbc.offset + cc * 515 + 3
        if qsl2 is not None:
            off += qsl2.start
            return bass.AP(xbc.tensor, off, [list(xbc.ap[0]), [1, qsl2.stop - qsl2.start]])
        return bass.AP(xbc.tensor, off, [list(xbc.ap[0]), [1, 512]])

    Bsl = xav(16)
    Csl = xav(17)

    w_xn_used = {}
    w_win_used = {}
    w_zgate = None
    w_xrt_done = None
    w_e4 = None
    w_gm4 = None
    w_uxread = None
    w_yc_last = None
    w_yprtp_last = None
    w_e4 = None
    w_dta_used = None

    for mg in range(NMEGA):
        tlo = mg * MEGA
        bb = mg % 2
        xnt = xnb[:, bb * NCH * 512:(bb + 1) * NCH * 512]
        w = [w_xn_used[mg - 2]] if mg >= 2 else []
        ld_xn = S.dma("sync", xnt, tile_view(c.xn_h, tlo, MEGA), waits=w, ch="an%d" % bb)
        w = [w_xrt_done] if w_xrt_done else []
        ld_xr = S.dma("sync", xrt, tile_view(c.xres, tlo, MEGA), waits=w, ch="ar")
        w = [w_dta_used] if mg > 0 else []
        S.dma("sync", dtm, c.dta_h[0][:, tlo:tlo + MEGA], waits=w, ch="dl")
        ld_dta = S.dma("sync", am, c.dta_h[1][:, tlo:tlo + MEGA], ch="dl")

        w_ipz = {}
        w_ipx = {}
        w_conv = {}
        w_dwmm = {}
        pend_conv = []
        ci_conv = [0]
        m_order = [32, 33] + list(range(16, 32)) + list(range(16))
        for mi, m in enumerate(m_order):
            wb = mi % 3
            wsl = winm[:, wb * NCH * 128:(wb + 1) * NCH * 128]
            gm = mg * 34 + mi
            w = [w_win_used[gm - 3]] if gm >= 3 else []
            ld_w = S.dma("sync", wsl, c.winb_h[m], waits=w, ch="aw%d" % wb)
            bk = "ABEF"[mi % 4]
            w = bgw(c, bk) + [ld_w, ld_xn]
            if mg == 0 and mi == 0:
                w.append(w_init)
            if w_zgate is not None and mi == 0:
                w.append(w_zgate)
            for kc in range(NCH):
                S.op("tensor", "matmul", pb[bk][:, 0:512],
                     wsl[:, kc * 128:(kc + 1) * 128], xnt[:, kc * 512:(kc + 1) * 512],
                     start=(kc == 0), stop=(kc == 7), waits=w if kc == 0 else [])
            w_mm = S.ms("tensor")
            w_win_used[gm] = w_mm
            if mi == len(m_order) - 1:
                w_xn_used[mg] = w_mm
            if m < 16:
                S.op("scalar", "activation", sz[:, m * 512:(m + 1) * 512],
                     pb[bk][:, 0:512], AF.Silu, waits=[w_mm])
                w_ipz[m] = S.ms("scalar")
                bgs(c, bk, w_ipz[m])
            else:
                cc = m - 16
                S.op("vector", "tensor_copy", xbc[:, cc * 515 + 3:cc * 515 + 515],
                     pb[bk][:, 0:512], waits=[w_mm])
                w_ipx[cc] = S.ms("vector")
                bgs(c, bk, w_ipx[cc])
            def drain_conv(keep):
                while len(pend_conv) > keep:
                    cc2p, basep, dslotp, dslp, (w_dwp, w_halop) = pend_conv.pop(0)
                    w = bgw(c, "D") + [w_dwp, w_halop]
                    for k in range(4):
                        S.op("tensor", "matmul", pb["D"][:, 0:512],
                             dslp[:, k * 128:(k + 1) * 128],
                             xbc[:, basep + k:basep + k + 512],
                             start=(k == 0), stop=(k == 3), waits=w if k == 0 else [])
                    w_dwmm[dslotp] = S.ms("tensor")
                    S.op("scalar", "activation",
                         bass.AP(xbc.tensor, xbc.offset + cc2p * 515 + 3,
                                 [list(xbc.ap[0]), [1, 512]]), pb["D"][:, 0:512],
                         AF.Silu, bias=c.convb_sb[:, cc2p:cc2p + 1],
                         waits=[w_dwmm[dslotp]])
                    w_conv[cc2p] = S.ms("scalar")
                    bgs(c, "D", w_conv[cc2p])
            drain_conv(2)
            if m >= 16:
                cc = m - 16
                base = cc * 515
                dslot = ci_conv[0] % 4
                ci_conv[0] += 1
                dsl2 = dw[:, dslot * 512:dslot * 512 + 512]
                S.op("vector", "tensor_copy", xbc[:, base:base + 3],
                     halo[:, cc * 3:(cc + 1) * 3], waits=[w_ipx[cc]])
                S.chain("vector")
                S.op("vector", "tensor_copy", halo[:, cc * 3:(cc + 1) * 3],
                     xbc[:, base + 512:base + 515])
                w_halo = S.ms("vector")
                w = [w_dwmm[dslot]] if dslot in w_dwmm else []
                for k in range(4):
                    S.op("vector", "tensor_scalar", dsl2[:, k * 128:(k + 1) * 128],
                         c.idb, c.convw_sb[:, cc * 4 + 3 - k:cc * 4 + 4 - k], None,
                         ALU.mult, waits=w if k == 0 else [])
                w_dw = S.ms("vector")
                pend_conv.append((cc, base, dslot, dsl2, (w_dw, w_halo)))
            if mi == len(m_order) - 1:
                drain_conv(0)

        if DBG_DUMP and mg == 0:
            dxbca = nc.dram_tensor("dbg_xbca", [128, NXBC * 512], BF16,
                                   kind="ExternalOutput").ap()
            dsz = nc.dram_tensor("dbg_sz", [128, 16 * 512], BF16,
                                 kind="ExternalOutput").ap()
            S.dma("sync", dxbca, bass.AP(xbc.tensor, xbc.offset + 3, [list(xbc.ap[0]), [515, NXBC], [1, 512]]), waits=[w_conv[cc2] for cc2 in range(NXBC)], ch="dbg")
            S.dma("sync", dsz, sz, waits=[w_ipz[mm2] for mm2 in range(16)], ch="dbg")
            S.barrier()
        w_y2 = {}
        w_lasttY = None
        w = bgw(c, "C") + [ld_dta]
        for qq2 in range(4):
            S.op("tensor", "transpose", pb["C"][:, qq2 * 32:(qq2 + 1) * 32],
                 am[:, qq2 * 128:(qq2 + 1) * 128], c.idf[0:32, 0:32],
                 waits=w if qq2 == 0 else [])
        w1 = S.ms("tensor")
        S.op("vector", "tensor_copy", aT, pb["C"][:, 0:128], waits=[w1])
        wv = S.ms("vector")
        S.op("tensor", "matmul", pb["C"][:, 128:256], c.tril, aT,
             start=True, stop=True, waits=[wv])
        S.op("tensor", "matmul", pb["C"][:, 256:384], c.trius, aT,
             start=True, stop=True)
        S.op("tensor", "matmul", pb["C"][0:1, 384:512], c.onescf, aT,
             start=True, stop=True)
        w2a = S.ms("tensor")
        w = bgw(c, "D")
        for qq2 in range(4):
            S.op("tensor", "transpose", pb["D"][:, qq2 * 32:(qq2 + 1) * 32],
                 dtm[:, qq2 * 128:(qq2 + 1) * 128], c.idf[0:32, 0:32],
                 waits=w if qq2 == 0 else [])
        w2 = S.ms("tensor")
        S.op("vector", "tensor_copy", cumcT, pb["C"][:, 128:256], waits=[w2a])
        S.op("vector", "tensor_copy", revT, pb["C"][:, 256:384])
        S.op("vector", "tensor_copy", dtT, pb["D"][:, 0:128], waits=[w2])
        wv = S.ms("vector")
        S.op("scalar", "activation", EcumT, cumcT, AF.Exp, waits=[wv])
        S.op("scalar", "activation", sc2T, revT, AF.Exp)
        ws = S.ms("scalar")
        S.op("vector", "tensor_mul", sc2T, sc2T, dtT, waits=[ws])
        w_sc2 = S.ms("vector")
        S.op("scalar", "activation", exl4, pb["C"][0:1, 384:512],
             AF.Exp, waits=[wv])
        w_exl = S.ms("scalar")
        S.op("tensor", "matmul", pb["D"][:, 128:256], c.onesrf, exl4,
             start=True, stop=True, waits=[w_exl])
        w3 = S.ms("tensor")
        S.op("vector", "tensor_copy", AtotB, pb["D"][:, 128:256], waits=[w3])
        w_prelim0 = S.ms("vector")
        bgs(c, "C", w_prelim0)
        bgs(c, "D", w_prelim0)
        if True:
            w_dta_used = w2
        for qq in range(4):
            ci = mg * 4 + qq
            hb = qq * 32
            qsl = slice(qq * 128, (qq + 1) * 128)
            S.op("vector", "tensor_copy", c.Sstb, c.Sst,
                 waits=[w_prelim0] if qq == 0 else [])
            w_prelim = S.ms("vector")
            w = bgw(c, "D") + [w_prelim, w_conv[16], w_conv[17]]
            S.op("tensor", "matmul", pb["D"][:, 0:128], xav(16, qsl), xav(17, qsl),
                 start=True, stop=True, waits=w)
            w4 = S.ms("tensor")
            S.op("vector", "tensor_mul", Gmsk, pb["D"][:, 0:128], c.mask01b, waits=[w4])
            bgs(c, "D", S.ms("vector"))
            w = bgw(c, "T1")
            S.op("tensor", "transpose", pb["T1"][:, 0:128], xav(16, qsl), c.idb, waits=w)
            w5 = S.ms("tensor")
            S.op("vector", "tensor_copy", BT, pb["T1"][:, 0:128], waits=[w5])
            bgs(c, "T1", S.ms("vector"))
            for bi, bkt in enumerate(("T1", "T2")):
                for sl8 in range(8):
                    c2 = bi * 8 + sl8
                    w = bgw(c, bkt) + [w_conv[c2]] if sl8 == 0 else [w_conv[c2]]
                    if w_uxread is not None and c2 == 0:
                        w.append(w_uxread)
                    S.op("tensor", "transpose", pb[bkt][:, sl8 * 128:(sl8 + 1) * 128],
                         xav(c2, qsl), c.idb, waits=w)
                w6 = S.ms("tensor")
                hb2 = bi * 16
                S.op("vector", "tensor_mul",
                     v3(Uall[:, hb2 * 64:(hb2 + 16) * 64], 16, 64),
                     v3(pb[bkt][:, 0:1024], 16, 64),
                     colrep(dtT, hb + hb2, 16), waits=[w6])
                S.op("vector", "tensor_mul",
                     v3(Xtall[:, hb2 * 64:(hb2 + 16) * 64], 16, 64),
                     v3(pb[bkt][:, 0:1024], 16, 64),
                     colrep(sc2T, hb + hb2, 16))
                bgs(c, bkt, S.ms("vector"))
            w_ux = S.now("vector")
            w_tY8 = {}
            for h8 in range(4):
                bkc = "C" if h8 % 2 == 0 else "D"
                for hh in range(8):
                    h = h8 * 8 + hh
                    w = bgw(c, bkc) if hh == 0 else []
                    if h == 0:
                        w = w + [w_prelim]
                        if w_yc_last is not None:
                            w.append(w_yc_last)
                    S.op("tensor", "matmul", pb[bkc][:, hh * 64:(hh + 1) * 64],
                         xav(17, qsl), c.Sstb[:, h * 64:(h + 1) * 64],
                         start=True, stop=True, waits=w)
                w8 = S.ms("tensor")
                S.op("vector", "tensor_mul",
                     v3(tYb[:, h8 * 512:(h8 + 1) * 512], 8, 64),
                     v3(pb[bkc][:, 0:512], 8, 64),
                     colrep(EcumT, hb + h8 * 8, 8), waits=[w8])
                w_tY8[h8] = S.ms("vector")
                bgs(c, bkc, w_tY8[h8])
                w_lasttY = w_tY8[h8]
            w_gm4g = {}
            for hg in range(5):
                if hg < 4:
                    for hh in range(8):
                        h = hg * 8 + hh
                        bk = "EFCD"[hh % 4]
                        slo = 128 + (hh // 4) * 128
                        w = bgw(c, bk) if hh < 4 else []
                        S.op("tensor", "transpose", pb[bk][:, slo:slo + 128],
                             fbc(cumcT[:, hb + h:hb + h + 1]), c.idf, waits=w)
                    w7 = S.ms("tensor")
                    for hh in range(8):
                        h = hg * 8 + hh
                        bk = "EFCD"[hh % 4]
                        slo = 128 + (hh // 4) * 128
                        w2 = ([w7] + ([w_e4] if w_e4 is not None else [])) if hh == 0 else []
                        S.op("scalar", "activation", D4[:, hh * 128:(hh + 1) * 128],
                             pb[bk][:, slo:slo + 128], AF.Relu,
                             bias=cumcT[:, hb + h:hb + h + 1], scale=-1.0, waits=w2)
                        if hh >= 4:
                            bgs(c, bk, S.ms("scalar"))
                        else:
                            S.ms("scalar")
                    w_d4 = S.now("scalar")
                    w = [w_d4] + ([w_gm4] if w_gm4 is not None else [])
                    S.op("scalar", "activation", E4, D4, AF.Exp, bias=0.0,
                         scale=-1.0, waits=w)
                    w_e4 = S.ms("scalar")
                    gsl = Gm4[:, (hg % 2) * 1024:(hg % 2) * 1024 + 1024]
                    w = [w_e4] + ([w_gm4g[hg - 2]] if hg >= 2 else [])
                    S.op("vector", "tensor_mul", v3(gsl, 8), v3(E4, 8),
                         rep_mid(Gmsk, 8), waits=w)
                    w_gm4 = S.ms("vector")
                    w_gm4g[hg] = w_gm4
                if hg >= 1:
                    pg = hg - 1
                    gsl = Gm4[:, (pg % 2) * 1024:(pg % 2) * 1024 + 1024]
                    bky = "A" if pg % 2 == 0 else "B"
                    for hh in range(8):
                        h = pg * 8 + hh
                        w = bgw(c, bky) + [w_gm4g[pg]] if hh == 0 else []
                        if h == 0 and w_yprtp_last is not None:
                            w.append(w_yprtp_last)
                        S.op("tensor", "matmul", pb[bky][:, hh * 64:(hh + 1) * 64],
                             gsl[:, hh * 128:(hh + 1) * 128],
                             Uall[:, h * 64:(h + 1) * 64], start=True, stop=True,
                             waits=w)
                    w_ymm = S.ms("tensor")
                    S.op("vector", "tensor_add",
                         ypra[:, pg * 512:(pg + 1) * 512],
                         tYb[:, pg * 512:(pg + 1) * 512], pb[bky][:, 0:512],
                         waits=[w_ymm, w_tY8[pg]])
                    w_yc_last = S.ms("vector")
                    bgs(c, bky, w_yc_last)
            for h8 in range(4):
                bk = "A" if h8 % 2 == 0 else "B"
                for hh in range(8):
                    h = h8 * 8 + hh
                    w = bgw(c, bk) if hh == 0 else []
                    S.op("tensor", "matmul", pb[bk][:, hh * 64:(hh + 1) * 64], BT,
                         Xtall[:, h * 64:(h + 1) * 64], start=True, stop=True,
                         waits=w)
                w10 = S.ms("tensor")
                ssl = c.Sst[:, h8 * 512:(h8 + 1) * 512]
                S.op("vector", "tensor_mul", v3(tmpS, 8, 64), v3(ssl, 8, 64),
                     colrep(AtotB, hb + h8 * 8, 8))
                S.chain("vector")
                S.op("vector", "tensor_add", ssl, tmpS, pb[bk][:, 0:512],
                     waits=[w10])
                bgs(c, bk, S.ms("vector"))
            for bi, bkt in enumerate(("T1", "T2")):
                for sl8 in range(8):
                    c2 = bi * 8 + sl8
                    w = bgw(c, bkt) if sl8 == 0 else []
                    S.op("tensor", "transpose", pb[bkt][:, sl8 * 128:(sl8 + 1) * 128],
                         ypra[:, c2 * 128:(c2 + 1) * 128], c.idb, waits=w)
                w11 = S.ms("tensor")
                w_yprtp_last = w11
                c2b = bi * 8
                xp8 = bass.AP(xbc.tensor, xbc.offset + c2b * 515 + 3 + qq * 128,
                              [list(xbc.ap[0]), [515, 8], [1, 128]])
                y28 = bass.AP(y2.tensor, y2.offset + c2b * 512 + qq * 128,
                              [list(y2.ap[0]), [512, 8], [1, 128]])
                S.op("vector", "tensor_mul", v3(t8w, 8), xp8,
                     colrep(c.drep_sb, c2b, 8, 128))
                S.chain("vector")
                S.op("vector", "tensor_add", y28, v3(t8w, 8),
                     bass.AP(pb[bkt].tensor, pb[bkt].offset,
                             [list(pb[bkt].ap[0]), [128, 8], [1, 128]]),
                     waits=[w11])
                w_y2[(qq, c2b)] = S.ms("vector")
                for sl8 in range(8):
                    w_y2[(qq, c2b + sl8)] = w_y2[(qq, c2b)]
                bgs(c, bkt, w_y2[(qq, c2b)])
            w_uxread = S.now("tensor")

        if DBG_DUMP and mg == 0:
            dy2 = nc.dram_tensor("dbg_y2", [128, 16 * 512], BF16,
                                 kind="ExternalOutput").ap()
            dst8 = nc.dram_tensor("dbg_S", [128, NH * HP], F32,
                                  kind="ExternalOutput").ap()
            ddt = nc.dram_tensor("dbg_dtm", [32, 512], F32,
                                 kind="ExternalOutput").ap()
            S.dma("sync", dy2, y2, waits=[w_y2[(3, cc2)] for cc2 in range(16)], ch="dbg")
            S.dma("sync", dst8, c.Sst, waits=[S.now("vector")], ch="dbg")
            S.dma("sync", ddt, dtm, ch="dbg")
            S.barrier()
        w_g = {}
        for c2 in range(16):
            S.op("vector", "tensor_mul", sz[:, c2 * 512:(c2 + 1) * 512],
                 y2[:, c2 * 512:(c2 + 1) * 512], sz[:, c2 * 512:(c2 + 1) * 512],
                 waits=[w_y2[(3, c2)], w_ipz[15]] if c2 == 0 else [w_y2[(3, c2)]])
            w_g[c2] = S.ms("vector")
        w_g2 = {}
        for c2 in range(16):
            gsl2 = g2t[:, (c2 % 2) * 512:(c2 % 2) * 512 + 512]
            w = [] if c2 < 2 else [w_g2mm[c2 - 2]]
            S.op("vector", "tensor_mul", gsl2, sz[:, c2 * 512:(c2 + 1) * 512],
                 sz[:, c2 * 512:(c2 + 1) * 512], waits=w)
            w_g2[c2] = S.ms("vector")
            w = bgw(c, "D") if c2 == 0 else []
            S.op("tensor", "matmul", pb["D"][0:1, 0:512], c.onescb, gsl2,
                 start=(c2 == 0), stop=(c2 == 15), waits=[w_g2[c2]] + w)
            if c2 == 0:
                w_g2mm = {}
            w_g2mm[c2] = S.ms("tensor")
        w_ss = w_g2mm[15]
        S.op("scalar", "activation", rsr, pb["D"][0:1, 0:512], AF.Ln,
             bias=c.ebias[0:1, 0:1], scale=1.0 / D_INNER, waits=[w_ss])
        bgs(c, "D", S.chain("scalar"))
        S.op("scalar", "activation", rsr, rsr, AF.Exp, bias=0.0, scale=-0.5)
        w_rsr = S.ms("scalar")
        w = bgw(c, "E") + [w_rsr]
        S.op("tensor", "matmul", pb["E"][:, 0:512], c.onesrf, rsr,
             start=True, stop=True, waits=w)
        w_rsb = S.ms("tensor")
        w_gn = {}
        for c2 in range(16):
            S.op("vector", "scalar_tensor_tensor", y2[:, c2 * 512:(c2 + 1) * 512],
                 sz[:, c2 * 512:(c2 + 1) * 512], c.normw_sb[:, c2:c2 + 1],
                 pb["E"][:, 0:512], ALU.mult, ALU.mult, waits=[w_rsb])
            w_gn[c2] = S.ms("vector")
        bgs(c, "E", w_gn[15])
        w_res = {}
        for mo in range(NCH):
            bk = "ABEF"[mo % 4]
            w0 = bgw(c, bk) + [ld_xr]
            for kc in range(16):
                if mo == 0:
                    w = (w0 if kc == 0 else []) + [w_gn[kc]]
                else:
                    w = (w0 + [w_gn[15]]) if kc == 0 else []
                S.op("tensor", "matmul", pb[bk][:, 0:512],
                     c.woutb[:, kc * 1024 + mo * 128:kc * 1024 + (mo + 1) * 128],
                     y2[:, kc * 512:(kc + 1) * 512],
                     start=(kc == 0), stop=(kc == 15), waits=w)
            w_mm = S.ms("tensor")
            if mo == NCH - 1:
                w_zgate = w_mm
            S.op("vector", "tensor_add", xrt[:, mo * 512:(mo + 1) * 512],
                 xrt[:, mo * 512:(mo + 1) * 512], pb[bk][:, 0:512], waits=[w_mm])
            w_res[mo] = S.ms("vector")
            bgs(c, bk, w_res[mo])
        st_x = S.dma("sync", tile_view(c.xres, tlo, MEGA), xrt,
                     waits=[w_res[NCH - 1], w_res[NCH - 2]], ch="as")
        w_stmm = {}
        w_cast = {}
        for cc in range(NCH):
            xsl = y2[:, cc * 512:(cc + 1) * 512]
            x2sl = y2[:, (NCH + cc) * 512:(NCH + cc + 1) * 512]
            S.op("vector", "tensor_copy", xsl, xrt[:, cc * 512:(cc + 1) * 512],
                 waits=[w_res[cc]] + ([w_zgate] if cc == 0 else []))
            S.chain("vector")
            S.op("vector", "tensor_mul", x2sl, xsl, xsl)
            w_cast[cc] = S.ms("vector")
        for cc in range(NCH):
            xsl = y2[:, cc * 512:(cc + 1) * 512]
            x2sl = y2[:, (NCH + cc) * 512:(NCH + cc + 1) * 512]
            w = ([w_cast[NCH - 1]] + bgw(c, "C") + [w_lasttY]) if cc == 0 else []
            S.op("tensor", "matmul", pb["C"][0:1, 0:512], c.onescb, xsl,
                 start=(cc == 0), stop=(cc == 7), waits=w)
            w2 = bgw(c, "D") if cc == 0 else []
            S.op("tensor", "matmul", pb["D"][0:1, 0:512], c.onescb, x2sl,
                 start=(cc == 0), stop=(cc == 7), waits=w2)
            w_stmm[cc] = S.ms("tensor")
        w_st = w_stmm[NCH - 1]
        w = [w_st, st_x] + ([w_r12] if mg > 0 else [])
        S.op("vector", "tensor_copy", c.r1, pb["C"][0:1, 0:512], waits=w)
        bgs(c, "C", S.ms("vector"))
        S.op("vector", "tensor_copy", c.r2, pb["D"][0:1, 0:512])
        w_xrt_done = S.ms("vector")
        bgs(c, "D", w_xrt_done)
        w_r12 = S.dma("sync", c.stats_h[0][None, tlo:tlo + MEGA], c.r1,
                      waits=[w_xrt_done], ch="ast")
        w_r12 = S.dma("sync", c.stats_h[1][None, tlo:tlo + MEGA], c.r2, ch="ast")

    stats_to_mrs(c, w_r12)
    stA.close()
    c.stW.close()


def stats_to_mrs(c, w_last, dim=DM):
    S = c.S
    w_pr = None
    for g in range(NMEGA):
        lo = g * MEGA
        ld1 = S.dma("sync", c.r1, c.stats_h[0][None, lo:lo + MEGA],
                    waits=[w_last] if g == 0 else [w_pr], ch="ml1")
        ld2 = S.dma("sync", c.r2, c.stats_h[1][None, lo:lo + MEGA], ch="ml2")
        S.op("scalar", "activation", c.r1, c.r1, AF.Copy, scale=1.0 / dim,
             waits=[ld1])
        S.chain("scalar")
        S.op("scalar", "activation", c.r3, c.r1, AF.Square)
        w_s = S.ms("scalar")
        S.op("vector", "scalar_tensor_tensor", c.r3, c.r2, 1.0 / dim, c.r3,
             ALU.mult, ALU.subtract, waits=[w_s, ld2])
        w_v = S.ms("vector")
        S.op("scalar", "activation", c.r3, c.r3, AF.Ln, bias=c.ebias[0:1, 0:1],
             scale=1.0, waits=[w_v])
        S.chain("scalar")
        S.op("scalar", "activation", c.r3, c.r3, AF.Exp, bias=0.0, scale=-0.5)
        w_rs = S.ms("scalar")
        S.dma("sync", c.stats_h[2][None, lo:lo + MEGA], c.r1, waits=[w_rs], ch="mrs")
        w_pr = S.dma("sync", c.stats_h[3][None, lo:lo + MEGA], c.r3, ch="mrs")
    c.w_lnrows = w_pr


def phaseB(c, tag):
    S, nc = c.S, c.nc
    pb = c.pbank
    stB = ExitStack()

    def al(nm, sh, dt):
        return stB.enter_context(nc.sbuf_tensor(nm + str(tag), sh, dt)).ap()

    wr_sb = al("wr_sb", [128, NCH * 1024], BF16)
    wk_sb = al("wk_sb", [128, NCH * 1024], BF16)
    wv_sb = al("wv_sb", [128, NCH * 1024], BF16)
    xrt = al("xrtB", [128, NCH * 512], F32)
    xn2 = al("xn2", [128, NCH * 513], BF16)
    mb_t = al("mb_t", [128, 512], BF16)
    rs_t = al("rs_t", [128, 512], BF16)
    t1b = al("t1b", [128, 2 * 512], BF16)
    dxb = al("dxb", [128, 2 * 512], BF16)
    r_t = al("r_t", [128, NCH * 512], BF16)
    ka_t = al("ka_t", [128, NCH * 512], BF16)
    kact = al("kact", [128, NCH * 512], BF16)
    kk_t = al("kk_t", [128, 2 * 512], BF16)
    o_t = al("o_t", [128, 2 * 512], F32)
    sig_t = al("sig_t", [128, NCH * 512], BF16)
    h1 = al("h1", [128, NCH], BF16)
    xb_t = al("xb_tB", [128, 2 * 512], BF16)
    x2b = al("x2bB", [128, 2 * 512], BF16)
    mrow = al("mrow", [1, 512], F32)
    rsrow = al("rsrow", [1, 512], F32)
    otm = al("otm", [128, 2 * DM], F32) if tag == 2 else None
    w_ost = {}

    wstg = al("wstg", [128, 4 * 1024], F32)
    w_wld = {}
    for k in range(24):
        wi, kc = k // NCH, k % NCH
        wsb = (wr_sb, wk_sb, wv_sb)[wi]
        wsrc = (c.wrT, c.wkT, c.wvT)[wi]
        b = k % 4
        sl = wstg[:, b * 1024:(b + 1) * 1024]
        w = [w_wld[((k - 4) // NCH, (k - 4) % NCH)]] if k >= 4 else []
        ld = S.dma("sync", sl, wsrc[kc * 128:(kc + 1) * 128, :], waits=w,
                   ch="bw%d" % b)
        S.op("scalar", "copy", wsb[:, kc * 1024:(kc + 1) * 1024], sl, waits=[ld])
        w_wld[(wi, kc)] = S.ms("scalar")
    w_wl = w_wld[(2, NCH - 1)]
    if tag == 1:
        c.wqkvb_h = nc.dram_tensor("wqkvb_h", [24, 128, NCH * 128], BF16,
                                   kind="Internal").ap()
        c.wpb_h = nc.dram_tensor("wpb_h", [NCH, 128, NCH * 128], BF16,
                                 kind="Internal").ap()
        wtmpC = al("wtmpC", [128, 2 * 1024], BF16)
        stq = {}
        for m in range(24):
            b = m % 2
            sl = wtmpC[:, b * 1024:(b + 1) * 1024]
            w = [stq[m - 2]] if m >= 2 else []
            ldw = S.dma("gpsimd", sl, kchunk_view(c.wqkv_l[m]), waits=w,
                        ch="qr%d" % b)
            stq[m] = S.dma("gpsimd", c.wqkvb_h[m], sl, waits=[ldw], ch="qs%d" % b)
        for m in range(NCH):
            b = m % 2
            sl = wtmpC[:, b * 1024:(b + 1) * 1024]
            src = bass.AP(c.wproj.tensor, c.wproj.offset + m * 128,
                          [[DM, 128], [128 * DM, NCH], [1, 128]])
            ldw = S.dma("gpsimd", sl, src, waits=[stq[22 + m]], ch="qr%d" % b)
            stq[24 + m] = S.dma("gpsimd", c.wpb_h[m], sl, waits=[ldw],
                                ch="qs%d" % b)
    S.op("vector", "memset", h1, 0.0)
    w_h1 = S.ms("vector")

    w_xrt_done = None
    w_vlast = None
    w_bc_prev = None
    st_x = None
    w_stmm = {}
    for mg in range(NMEGA):
        tlo = mg * MEGA
        w = ([w_xrt_done] if w_xrt_done else []) + ([st_x] if st_x else [])
        ld_xr = S.dma("sync", xrt, tile_view(c.xres, tlo, MEGA), waits=w, ch="br")
        w = [c.w_lnrows] if mg == 0 else [w_bc_prev]
        ld_m = S.dma("sync", mrow, c.stats_h[2][None, tlo:tlo + MEGA], waits=w, ch="bm1")
        ld_rs = S.dma("sync", rsrow, c.stats_h[3][None, tlo:tlo + MEGA], ch="bm2")
        w = bgw(c, "C") + [ld_m] + ([w_h1] if mg == 0 else [])
        S.op("tensor", "matmul", pb["C"][:, 0:512], c.onesrf, mrow,
             start=True, stop=True, waits=w)
        w = bgw(c, "D") + [ld_rs]
        S.op("tensor", "matmul", pb["D"][:, 0:512], c.onesrf, rsrow,
             start=True, stop=True, waits=w)
        w_bc = S.ms("tensor")
        w_bc_prev = w_bc
        S.op("vector", "tensor_copy", mb_t, pb["C"][:, 0:512], waits=[w_bc])
        bgs(c, "C", S.now("vector"))
        S.op("vector", "tensor_copy", rs_t, pb["D"][:, 0:512])
        bgs(c, "D", S.chain("vector"))
        w_mix = {}
        for cc in range(NCH):
            xsl = xn2[:, cc * 513:cc * 513 + 513]
            tsl = t1b[:, (cc % 2) * 512:(cc % 2) * 512 + 512]
            dsl = dxb[:, (cc % 2) * 512:(cc % 2) * 512 + 512]
            w = [ld_xr]
            if w_vlast is not None and cc == 0:
                w.append(w_vlast)
            S.op("vector", "tensor_copy", xsl[:, 0:1], h1[:, cc:cc + 1], waits=w)
            S.chain("vector")
            S.op("vector", "tensor_sub", tsl, xrt[:, cc * 512:(cc + 1) * 512], mb_t)
            S.chain("vector")
            S.op("vector", "tensor_mul", xsl[:, 1:513], tsl, rs_t)
            S.chain("vector")
            S.op("vector", "tensor_sub", dsl, xsl[:, 1:513], xsl[:, 0:512])
            S.chain("vector")
            S.op("vector", "scalar_tensor_tensor", tsl, dsl, c.mur_sb[:, cc:cc + 1],
                 xsl[:, 0:512], ALU.mult, ALU.add)
            S.chain("vector")
            S.op("vector", "tensor_copy", r_t[:, cc * 512:(cc + 1) * 512], tsl)
            S.chain("vector")
            S.op("vector", "scalar_tensor_tensor", ka_t[:, cc * 512:(cc + 1) * 512],
                 dsl, c.muk_sb[:, cc:cc + 1], xsl[:, 0:512], ALU.mult, ALU.add)
            w_mix[cc] = S.ms("vector")
            S.op("vector", "tensor_copy", h1[:, cc:cc + 1], xsl[:, 512:513])
            S.chain("vector")
        w_mixall = S.now("vector")
        w_rmm = {}
        w_ka = {}
        for mo in range(NCH):
            bk = "AE"[mo % 2]
            w = bgw(c, bk) + ([w_mixall] if mo == 0 else [])
            for kc in range(NCH):
                wk2 = (w if kc == 0 else []) + \
                    ([w_wld[(0, kc)]] if mg == 0 else [])
                S.op("tensor", "matmul", pb[bk][:, 0:512],
                     wr_sb[:, kc * 1024 + mo * 128:kc * 1024 + (mo + 1) * 128],
                     r_t[:, kc * 512:(kc + 1) * 512],
                     start=(kc == 0), stop=(kc == 7), waits=wk2)
            w_rmm[mo] = S.ms("tensor")
            S.op("scalar", "activation", sig_t[:, mo * 512:(mo + 1) * 512],
                 pb[bk][:, 0:512], AF.Sigmoid, waits=[w_rmm[mo]])
            bgs(c, bk, S.ms("scalar"))
            bk = "BF"[mo % 2]
            w = bgw(c, bk) + ([w_mixall] if mo == 0 else [])
            for kc in range(NCH):
                wk2 = (w if kc == 0 else []) + \
                    ([w_wld[(1, kc)]] if mg == 0 else [])
                S.op("tensor", "matmul", pb[bk][:, 0:512],
                     wk_sb[:, kc * 1024 + mo * 128:kc * 1024 + (mo + 1) * 128],
                     ka_t[:, kc * 512:(kc + 1) * 512],
                     start=(kc == 0), stop=(kc == 7), waits=wk2)
            w_kmm = S.ms("tensor")
            ksl = kk_t[:, (mo % 2) * 512:(mo % 2) * 512 + 512]
            w = [w_kmm] + ([w_ka[mo - 2]] if mo >= 2 else [])
            S.op("vector", "tensor_scalar", ksl, pb[bk][:, 0:512], 0.0, None,
                 ALU.max, waits=w)
            bgs(c, bk, S.chain("vector"))
            w = [w_vlast] if (mo == 0 and w_vlast is not None) else []
            S.op("vector", "tensor_mul", kact[:, mo * 512:(mo + 1) * 512], ksl, ksl,
                 waits=w)
            w_ka[mo] = S.ms("vector")
        w_sig = S.now("scalar")
        w_res = {}
        for mo in range(NCH):
            bk = "CD"[mo % 2]
            w0 = bgw(c, bk)
            for kc in range(NCH):
                if mo == 0:
                    w = (w0 if kc == 0 else []) + [w_ka[kc]]
                else:
                    w = (w0 + [w_ka[NCH - 1]]) if kc == 0 else []
                if mg == 0:
                    w = w + [w_wld[(2, kc)]]
                S.op("tensor", "matmul", pb[bk][:, 0:512],
                     wv_sb[:, kc * 1024 + mo * 128:kc * 1024 + (mo + 1) * 128],
                     kact[:, kc * 512:(kc + 1) * 512],
                     start=(kc == 0), stop=(kc == 7), waits=w)
            w_vmm = S.ms("tensor")
            w_vlast = w_vmm
            osl = o_t[:, (mo % 2) * 512:(mo % 2) * 512 + 512]
            w = [w_vmm, w_sig] + ([w_res[mo - 2]] if mo >= 2 else [])
            S.op("vector", "tensor_mul", osl, sig_t[:, mo * 512:(mo + 1) * 512],
                 pb[bk][:, 0:512], waits=w)
            bgs(c, bk, S.chain("vector"))
            S.op("vector", "tensor_add", xrt[:, mo * 512:(mo + 1) * 512],
                 xrt[:, mo * 512:(mo + 1) * 512], osl)
            w_res[mo] = S.ms("vector")
        if DBG_DUMP and tag == 2 and mg == 0:
            dxn2 = nc.dram_tensor("dbg_xn2", [128, NCH * 513], BF16,
                                  kind="ExternalOutput").ap()
            dsig = nc.dram_tensor("dbg_sig", [128, NCH * 512], BF16,
                                  kind="ExternalOutput").ap()
            dka = nc.dram_tensor("dbg_ka", [128, NCH * 512], BF16,
                                 kind="ExternalOutput").ap()
            dxr2 = nc.dram_tensor("dbg_xr2", [128, NCH * 512], F32,
                                  kind="ExternalOutput").ap()
            S.dma("sync", dxn2, xn2, waits=[w_res[NCH - 1]], ch="dbg")
            S.dma("sync", dsig, sig_t, ch="dbg")
            S.dma("sync", dka, r_t, ch="dbg")
            S.dma("sync", dxr2, xrt, ch="dbg")
            S.barrier()
        if tag == 2:
            for tt in range(4):
                for cc2 in range(NCH):
                    bkf = "EFCD"[cc2 % 4]
                    w = bgw(c, bkf) + ([w_res[cc2]] if tt == 0 else [])
                    S.op("tensor", "transpose", pb[bkf][:, 0:128],
                         xrt[:, cc2 * 512 + tt * 128:cc2 * 512 + (tt + 1) * 128],
                         c.idf, waits=w)
                    w_tp = S.ms("tensor")
                    w2 = [w_tp] + ([w_ost[tt % 2]] if (cc2 == 0 and (tt % 2) in w_ost)
                                   else [])
                    S.op("scalar", "copy",
                         otm[:, (tt % 2) * DM + cc2 * 128:(tt % 2) * DM + (cc2 + 1) * 128],
                         pb[bkf][:, 0:128], waits=w2)
                    bgs(c, bkf, S.ms("scalar"))
                w_sc = S.now("scalar")
                w_ost[tt % 2] = S.dma(
                    "sync", c.y_out[tlo + tt * 128:tlo + (tt + 1) * 128, :],
                    otm[:, (tt % 2) * DM:(tt % 2 + 1) * DM], waits=[w_sc],
                    ch="es%d" % (tt % 2))
            st_x = w_ost[(4 - 1) % 2]
        else:
            st_x = S.dma("sync", tile_view(c.xres, tlo, MEGA), xrt,
                         waits=[w_res[NCH - 1], w_res[NCH - 2]], ch="bst")
        if tag == 1:
            for cc in range(NCH):
                xsl2 = xb_t[:, (cc % 2) * 512:(cc % 2) * 512 + 512]
                x2sl = x2b[:, (cc % 2) * 512:(cc % 2) * 512 + 512]
                w = [w_res[cc]] + ([w_stmm[cc - 2]] if cc >= 2 else [])
                S.op("vector", "tensor_copy", xsl2, xrt[:, cc * 512:(cc + 1) * 512],
                     waits=w)
                S.chain("vector")
                S.op("vector", "tensor_mul", x2sl, xsl2, xsl2)
                wp = S.ms("vector")
                w = bgw(c, "C") if cc == 0 else []
                S.op("tensor", "matmul", pb["C"][0:1, 0:512], c.onescb, xsl2,
                     start=(cc == 0), stop=(cc == 7), waits=[wp] + w)
                w2 = bgw(c, "D") if cc == 0 else []
                S.op("tensor", "matmul", pb["D"][0:1, 0:512], c.onescb, x2sl,
                     start=(cc == 0), stop=(cc == 7), waits=w2)
                w_stmm[cc] = S.ms("tensor")
            w_st = w_stmm[NCH - 1]
            w = [w_st, st_x] + ([c.w_r12B] if mg > 0 else [])
            S.op("vector", "tensor_copy", c.r1, pb["C"][0:1, 0:512], waits=w)
            bgs(c, "C", S.ms("vector"))
            S.op("vector", "tensor_copy", c.r2, pb["D"][0:1, 0:512])
            w_xrt_done = S.ms("vector")
            bgs(c, "D", w_xrt_done)
            c.w_r12B = S.dma("sync", c.stats_h[0][None, tlo:tlo + MEGA], c.r1,
                             waits=[w_xrt_done], ch="bss")
            c.w_r12B = S.dma("sync", c.stats_h[1][None, tlo:tlo + MEGA], c.r2, ch="bss")
        else:
            w_xrt_done = w_res[NCH - 1]
    stB.close()


def phaseC(c):
    S, nc = c.S, c.nc
    pb = c.pbank
    stC = ExitStack()

    def al(nm, sh, dt):
        return stC.enter_context(nc.sbuf_tensor(nm, sh, dt)).ap()

    stats_to_mrs(c, None)

    K_sb = al("K_sb", [128, NCH * T], BF16)
    vb = al("vb", [128, 32 * 1040], BF16)
    permb = al("permb", [128, 128], BF16)
    battnb = al("battnb", [1, 24 * 128], BF16)
    bprojb = al("bprojb", [1, NCH * 128], BF16)
    ones512 = al("ones512", [1, 512], BF16)
    wq_sb = al("wq_sb", [128, 3 * NCH * 128], BF16)
    xrt = al("xrtC", [128, NCH * 512], F32)
    st1 = ExitStack()

    def al1(nm, sh, dt):
        return st1.enter_context(nc.sbuf_tensor(nm, sh, dt)).ap()

    cos_sb = al1("cos_sb", [128, 2 * 512], BF16)
    sin_sb = al1("sin_sb", [128, 2 * 512], BF16)
    xn3 = al1("xn3", [128, NCH * 512], BF16)
    q_st = al1("q_st", [128, NCH * 512], BF16)
    mb_t = al1("mb_tC", [128, 512], BF16)
    rs_t = al1("rs_tC", [128, 512], BF16)
    tr1 = al1("tr1", [128, 2 * 512], BF16)
    tr2 = al1("tr2", [128, 2 * 512], BF16)
    tr3 = al1("tr3", [128, 2 * 512], BF16)
    mrow = al1("mrowC", [1, 512], F32)
    rsrow = al1("rsrowC", [1, 512], F32)

    w_pc = S.dma("gpsimd", permb, c.c_perm, ch="cg")
    S.dma("gpsimd", battnb, c.battn_r, ch="cg")
    w_bb = S.dma("gpsimd", bprojb, c.bproj_r, ch="cg")
    S.op("vector", "memset", ones512, 1.0)
    w_o512 = S.ms("vector")
    wqkvb_h = c.wqkvb_h
    wpb_h = c.wpb_h

    S.op("vector", "memset",
         bass.AP(vb.tensor, vb.offset + 64, [list(vb.ap[0]), [65, 16 * 32], [1, 1]]),
         1.0)
    w_vones = S.ms("vector")

    w_xrt_done = None
    st_q = None
    w_win_used = {}
    w_kvdone = []
    w_bc_prev = None
    w_rope_slot = {}
    for mg in range(NMEGA):
        tlo = mg * MEGA
        w = ([w_xrt_done] if w_xrt_done else [])
        ld_xr = S.dma("sync", xrt, tile_view(c.xres, tlo, MEGA), waits=w, ch="cr")
        w = [c.w_lnrows] if mg == 0 else [w_bc_prev]
        ld_m = S.dma("sync", mrow, c.stats_h[2][None, tlo:tlo + MEGA], waits=w, ch="cm1")
        ld_rs = S.dma("sync", rsrow, c.stats_h[3][None, tlo:tlo + MEGA], ch="cm2")
        S.dma("gpsimd", cos_sb[:, (mg % 2) * 512:(mg % 2) * 512 + 512],
              c.c_cos[:, tlo:tlo + MEGA], waits=[S.now("vector")] if mg >= 2 else [],
              ch="ct%d" % (mg % 2))
        ld_cs = S.dma("gpsimd", sin_sb[:, (mg % 2) * 512:(mg % 2) * 512 + 512],
              c.c_sin[:, tlo:tlo + MEGA], ch="ct%d" % (mg % 2))
        cssl = cos_sb[:, (mg % 2) * 512:(mg % 2) * 512 + 512]
        snsl = sin_sb[:, (mg % 2) * 512:(mg % 2) * 512 + 512]
        w = bgw(c, "C") + [ld_m]
        S.op("tensor", "matmul", pb["C"][:, 0:512], c.onesrf, mrow,
             start=True, stop=True, waits=w)
        w = bgw(c, "D") + [ld_rs]
        S.op("tensor", "matmul", pb["D"][:, 0:512], c.onesrf, rsrow,
             start=True, stop=True, waits=w)
        w_bc = S.ms("tensor")
        w_bc_prev = w_bc
        S.op("vector", "tensor_copy", mb_t, pb["C"][:, 0:512], waits=[w_bc])
        bgs(c, "C", S.now("vector"))
        S.op("vector", "tensor_copy", rs_t, pb["D"][:, 0:512])
        bgs(c, "D", S.chain("vector"))
        w_xn = {}
        for cc in range(NCH):
            tsl = tr1[:, (cc % 2) * 512:(cc % 2) * 512 + 512]
            w = [ld_xr]
            if mg > 0 and cc == 0:
                w.append(w_qkv_used)
            S.op("vector", "tensor_sub", tsl, xrt[:, cc * 512:(cc + 1) * 512], mb_t,
                 waits=w)
            S.chain("vector")
            S.op("vector", "tensor_mul", xn3[:, cc * 512:(cc + 1) * 512], tsl, rs_t)
            w_xn[cc] = S.chain("vector")
        w_xnall = S.now("vector")
        for m in range(24):
            b = m % 3
            wsl = wq_sb[:, b * 1024:(b + 1) * 1024]
            gm = mg * 24 + m
            w = [w_win_used[gm - 3]] if gm >= 3 else []
            ld_w = S.dma("sync", wsl, wqkvb_h[m], waits=w, ch="cw%d" % b)
            bk = "AB"[m % 2]
            w = bgw(c, bk) + [ld_w, w_xnall]
            if gm == 0:
                w += [w_bb, w_o512]
            for kc in range(NCH):
                S.op("tensor", "matmul", pb[bk][:, 0:512],
                     wsl[:, kc * 128:(kc + 1) * 128], xn3[:, kc * 512:(kc + 1) * 512],
                     start=(kc == 0), stop=False, waits=w if kc == 0 else [])
            S.op("tensor", "matmul", pb[bk][:, 0:512],
                 battnb[0:1, m * 128:(m + 1) * 128], ones512,
                 start=False, stop=True)
            w_mm = S.ms("tensor")
            w_win_used[gm] = w_mm
            cq = m % 8
            if m < 16:
                qtmp = tr2[:, (m % 2) * 512:(m % 2) * 512 + 512]
                w = [w_mm] + ([w_rope_slot[m % 2]] if (m % 2) in w_rope_slot else [])
                S.op("scalar", "activation", qtmp, pb[bk][:, 0:512], AF.Copy,
                     waits=w)
                w_ev = S.ms("scalar")
                bgs(c, bk, w_ev)
                bkp = "E" if m % 2 == 0 else "F"
                w = bgw(c, bkp) + [w_ev]
                S.op("tensor", "matmul", pb[bkp][:, 0:512], permb, qtmp,
                     start=True, stop=True, waits=w)
                w_pm = S.ms("tensor")
                tsl = tr1[:, (m % 2) * 512:(m % 2) * 512 + 512]
                t2sl = tr3[:, (m % 2) * 512:(m % 2) * 512 + 512]
                w = [ld_cs, w_ev]
                if m == 0 and st_q is not None:
                    w.append(st_q)
                S.op("vector", "tensor_mul", tsl, qtmp, cssl, waits=w)
                S.chain("vector")
                S.op("vector", "tensor_mul", t2sl, pb[bkp][:, 0:512], snsl,
                     waits=[w_pm])
                S.chain("vector")
                dst = (q_st[:, cq * 512:(cq + 1) * 512] if m < 8 else
                       K_sb[:, cq * T + tlo:cq * T + tlo + MEGA])
                S.op("vector", "tensor_add", dst, tsl, t2sl)
                w_rope = S.ms("vector")
                w_rope_slot[m % 2] = w_rope
                bgs(c, bkp, w_rope)
            else:
                vtmp = tr2[:, (m % 2) * 512:(m % 2) * 512 + 512]
                w = [w_mm] + ([w_rope_slot[m % 2]] if (m % 2) in w_rope_slot else [])
                S.op("scalar", "activation", vtmp, pb[bk][:, 0:512], AF.Copy,
                     waits=w)
                w_ev = S.ms("scalar")
                bgs(c, bk, w_ev)
                for tt in range(4):
                    bkt = "T1" if tt % 2 == 0 else "T2"
                    kt = tlo // 128 + tt
                    w = bgw(c, bkt) + ([w_ev] if tt < 2 else [])
                    S.op("tensor", "transpose", pb[bkt][:, 0:128],
                         vtmp[:, tt * 128:(tt + 1) * 128], c.idb, waits=w)
                    w_tp = S.ms("tensor")
                    dst = bass.AP(vb.tensor, vb.offset + kt * 1040 + (2 * cq) * 65,
                                  [list(vb.ap[0]), [65, 2], [1, 64]])
                    src = bass.AP(pb[bkt].tensor, pb[bkt].offset,
                                  [list(pb[bkt].ap[0]), [64, 2], [1, 64]])
                    S.op("vector", "tensor_copy", dst, src,
                         waits=[w_tp] + ([w_vones] if mg == 0 and m == 16 and tt == 0 else []))
                    bgs(c, bkt, S.ms("vector"))
        w_qkv_used = S.now("tensor")
        w_kvdone = [S.now("vector"), S.now("scalar")]
        w = [S.now("vector")] + ([st_q] if st_q else [])
        st_q = S.dma("sync", tile_view(c.q_hbm, tlo, MEGA), q_st, waits=w, ch="cq")
        w_xrt_done = w_qkv_used

    w_c1done = [S.now("tensor"), S.now("vector"), S.now("scalar")]
    if DBG_DUMP:
        dK = nc.dram_tensor("dbg_K", [128, NCH * T], BF16, kind="ExternalOutput").ap()
        dvb = nc.dram_tensor("dbg_vb", [128, 32 * 1040], BF16, kind="ExternalOutput").ap()
        dq = nc.dram_tensor("dbg_q", [NCH, 128, T], BF16, kind="ExternalOutput").ap()
        S.dma("sync", dK, K_sb, waits=w_c1done + [st_q], ch="dbg")
        S.dma("sync", dvb, vb, ch="dbg")
        S.dma("sync", dq, c.q_hbm, ch="dbg")
        S.barrier()
    st1.close()
    qq_sb = al("qq_sb", [128, 2 * NCH * 512], BF16)
    pt_sb = al("pt_sb", [128, 6 * 512], BF16)
    yraw = al("yraw", [128, NCH * 512], BF16)
    den_t = al("den_t", [1, 2 * 512], F32)
    den16 = al("den16", [128, 64], F32)
    denb_t = al("denb_t", [1, 2 * 512], BF16)
    xb_t = yraw[:, 0:2 * 512]
    x2b = yraw[:, 2 * 512:4 * 512]
    denb_h = c.denb_h

    w_den_st = {}
    w_stmm = {}
    w_oslot = {}
    st_db = None
    st_x = None
    w_rb = None
    ld_d16_prev = None
    w_xrt2 = None
    w_pv_slot = {}
    w_wp_used = {}
    w_q_used = {}
    for qm in range(NMEGA):
        qlo = qm * MEGA
        qsl = qq_sb[:, (qm % 2) * NCH * 512:(qm % 2 + 1) * NCH * 512]
        w = [st_q] + (w_c1done if qm == 0 else []) + \
            ([w_q_used[qm - 2]] if qm >= 2 else [])
        ld_qq = S.dma("sync", qsl, tile_view(c.q_hbm, qlo, MEGA), waits=w, ch="c2q")
        w = ([w_xrt2] if w_xrt2 else [w_xrt_done]) + ([st_x] if st_x else [])
        ld_xr = S.dma("sync", xrt, tile_view(c.xres, qlo, MEGA), waits=w, ch="cr")
        nkt = 4 * qm + 4
        for cc in range(NCH):
            h0 = 2 * cc
            w_exp = {}
            pvargs = {}
            for kt in range(nkt + 1):
                if kt < nkt:
                    off = max(0, (kt - 4 * qm) * 128)
                    N = 512 - off
                    s0 = 2 * (kt % 3)
                    pair = "EF" if kt % 2 == 0 else "CD"
                    for half in range(2):
                        bks = ("E", "F", "C", "D")[2 * (kt % 2) + half]
                        rows = slice(half * 64, half * 64 + 64)
                        w = bgw(c, bks)
                        if cc == 0 and kt == 0 and qm == 0:
                            w += w_kvdone + [ld_qq]
                        if kt <= 1:
                            w += [ld_qq]
                        S.op("tensor", "matmul", pb[bks][:, 0:N],
                             K_sb[rows, cc * T + kt * 128:cc * T + (kt + 1) * 128],
                             qsl[rows, cc * 512 + off:(cc + 1) * 512],
                             start=True, stop=True,
                             tile_position=(half * 64, 0), waits=w)
                    w_s = S.ms("tensor")
                    wl = [w_s]
                    for half in range(2):
                        if (kt % 3, half) in w_pv_slot:
                            wl.append(w_pv_slot[(kt % 3, half)])
                    if off <= 128:
                        S.op("scalar", "activation",
                             pt_sb[:, s0 * 512:s0 * 512 + 512 + N],
                             pb[pair][:, 0:512 + N], AF.Exp,
                             bias=c.ebias[:, 1:2], scale=0.125, waits=wl)
                        w_es = [S.ms("scalar")] * 2
                    else:
                        w_es = []
                        for half in range(2):
                            S.op("scalar", "activation",
                                 pt_sb[:, (s0 + half) * 512:(s0 + half) * 512 + N],
                                 pb[pair][:, half * 512:half * 512 + N], AF.Exp,
                                 bias=c.ebias[:, 1:2], scale=0.125,
                                 waits=wl if half == 0 else [])
                            w_es.append(S.ms("scalar"))
                    bgs(c, pair[0], w_es[1])
                    bgs(c, pair[1], w_es[1])
                    for half in range(2):
                        psl = pt_sb[:, (s0 + half) * 512:(s0 + half) * 512 + N]
                        w_e = w_es[half]
                        if kt >= 4 * qm:
                            S.op("vector", "tensor_mul", psl[:, 0:128], psl[:, 0:128],
                                 c.mask01b, waits=[w_e])
                            w_e = S.ms("vector")
                        w_exp[(kt, half)] = w_e
                        pvargs[(kt, half)] = (off, psl)
                if kt >= 1:
                    pkt = kt - 1
                    for half in range(2):
                        h = h0 + half
                        bky = "A" if half == 0 else "B"
                        off, psl = pvargs.pop((pkt, half))
                        wpv = [w_exp.pop((pkt, half))]
                        if pkt == 0:
                            wpv += bgw(c, bky)
                        S.op("tensor", "matmul", pb[bky][0:65, off:512],
                             vb[:, pkt * 1040 + h * 65:pkt * 1040 + h * 65 + 65],
                             psl, start=(pkt == 0), stop=(pkt == nkt - 1),
                             skip_group_check=True, waits=wpv)
                        w_pv_slot[(pkt % 3, half)] = S.ms("tensor")
            w_pvA = w_pv_slot.get(((nkt - 1) % 3, 0))
            w_pvB = w_pv_slot.get(((nkt - 1) % 3, 1))
            w_lastS = w_pvB
            w = [w_pvA, w_pvB]
            if cc >= 1:
                w.append(w_den_st.get(cc - 1))
            if cc == 0 and ld_d16_prev is not None:
                w.append(ld_d16_prev)
            S.op("vector", "tensor_copy", yraw[0:64, cc * 512:(cc + 1) * 512],
                 pb["A"][0:64, 0:512], waits=w)
            S.op("vector", "tensor_copy", yraw[64:128, cc * 512:(cc + 1) * 512],
                 pb["B"][0:64, 0:512])
            S.op("vector", "tensor_copy", den_t[0:1, 0:512], pb["A"][64:65, 0:512])
            S.op("vector", "tensor_copy", den_t[0:1, 512:1024], pb["B"][64:65, 0:512])
            w_dv = S.ms("vector")
            bgs(c, "A", w_dv)
            bgs(c, "B", w_dv)
            S.dma("sync", c.den_h[2 * cc], den_t[0:1, 0:512], waits=[w_dv], ch="cd")
            w_den_st[cc] = S.dma("sync", c.den_h[2 * cc + 1], den_t[0:1, 512:1024], ch="cd")
        if DBG_DUMP and qm == 1:
            dyr = nc.dram_tensor("dbg_yraw", [128, NCH * 512], BF16,
                                 kind="ExternalOutput").ap()
            dden = nc.dram_tensor("dbg_den", [16, 512], F32,
                                  kind="ExternalOutput").ap()
            S.dma("sync", dyr, yraw, waits=[S.now("vector")], ch="dbg")
            S.dma("sync", dden, c.den_h, ch="dbg")
            S.barrier()
        w = [w_den_st[NCH - 1]] + ([st_db] if qm > 0 else [])
        ld_d16 = S.dma("sync", den16, c.den_h, waits=w, ch="cdl")
        ld_d16_prev = ld_d16
        S.op("vector", "reciprocal", den16, den16, waits=[ld_d16])
        w_rc = S.ms("vector")
        st_db = S.dma("gpsimd", denb_h, den16, waits=[w_rc], ch="cdb")
        w_res = {}
        w_norm = {}
        w_ores = {}
        for cc in range(NCH):
            w = [st_db] + ([w_rb] if (qm + cc) > 0 else [])
            ldb = S.dma("sync", denb_t[0:1, 0:512],
                        c.denb_h[0:1, (2 * cc) * 512:(2 * cc + 1) * 512], waits=w, ch="cdn")
            w_ldb = S.dma("sync", denb_t[0:1, 512:1024],
                          c.denb_h[0:1, (2 * cc + 1) * 512:(2 * cc + 2) * 512], ch="cdn")
            rbk = "AB"[cc % 2]
            w = bgw(c, rbk) + [ldb, w_ldb]
            S.op("tensor", "matmul", pb[rbk][0:64, 0:512], c.onesrb[0:1, 0:64],
                 denb_t[0:1, 0:512], start=True, stop=True, waits=w)
            S.op("tensor", "matmul", pb[rbk][64:128, 0:512], c.onesrb[0:1, 0:64],
                 denb_t[0:1, 512:1024], start=True, stop=True,
                 tile_position=(0, 64))
            w_rb = S.ms("tensor")
            S.op("vector", "tensor_mul", yraw[:, cc * 512:(cc + 1) * 512],
                 yraw[:, cc * 512:(cc + 1) * 512], pb[rbk][:, 0:512],
                 waits=[w_rb])
            w_norm[cc] = S.ms("vector")
            bgs(c, rbk, w_norm[cc])
        for mo in range(NCH):
            b = mo % 3
            wsl = wq_sb[:, b * 1024:(b + 1) * 1024]
            gm2 = qm * NCH + mo
            w = [w_wp_used[gm2 - 3]] if gm2 >= 3 else []
            ld_w = S.dma("sync", wsl, wpb_h[mo], waits=w, ch="cw%d" % b)
            bk = "AB"[mo % 2]
            w0 = bgw(c, bk) + [ld_w, ld_xr]
            for kc in range(NCH):
                if mo == 0:
                    w = (w0 if kc == 0 else []) + [w_norm[kc]]
                else:
                    w = (w0 + [w_norm[NCH - 1]]) if kc == 0 else []
                S.op("tensor", "matmul", pb[bk][:, 0:512],
                     wsl[:, kc * 128:(kc + 1) * 128],
                     yraw[:, kc * 512:(kc + 1) * 512],
                     start=(kc == 0), stop=False, waits=w)
            S.op("tensor", "matmul", pb[bk][:, 0:512],
                 bprojb[0:1, mo * 128:(mo + 1) * 128], ones512,
                 start=False, stop=True)
            w_mm = S.ms("tensor")
            w_wp_used[gm2] = w_mm
            S.op("vector", "tensor_add", xrt[:, mo * 512:(mo + 1) * 512],
                 xrt[:, mo * 512:(mo + 1) * 512], pb[bk][:, 0:512], waits=[w_mm])
            w_ores[mo] = S.ms("vector")
            bgs(c, bk, w_ores[mo])
        st_x = S.dma("sync", tile_view(c.xres, qlo, MEGA), xrt,
                     waits=[w_ores[NCH - 1], w_ores[NCH - 2]], ch="cst")
        for cc in range(NCH):
            xsl2 = xb_t[:, (cc % 2) * 512:(cc % 2) * 512 + 512]
            x2sl = x2b[:, (cc % 2) * 512:(cc % 2) * 512 + 512]
            w = [w_ores[cc]] + ([w_stmm[cc - 2]] if cc >= 2 else [])
            if cc == 0:
                w.append(w_wp_used[qm * NCH + NCH - 1])
            S.op("vector", "tensor_copy", xsl2, xrt[:, cc * 512:(cc + 1) * 512],
                 waits=w)
            S.chain("vector")
            S.op("vector", "tensor_mul", x2sl, xsl2, xsl2)
            wp2 = S.ms("vector")
            w = bgw(c, "A") if cc == 0 else []
            S.op("tensor", "matmul", pb["A"][0:1, 0:512], c.onescb, xsl2,
                 start=(cc == 0), stop=(cc == 7), waits=[wp2] + w)
            w2 = bgw(c, "B") if cc == 0 else []
            S.op("tensor", "matmul", pb["B"][0:1, 0:512], c.onescb, x2sl,
                 start=(cc == 0), stop=(cc == 7), waits=w2)
            w_stmm[cc] = S.ms("tensor")
        w_st = w_stmm[NCH - 1]
        w = [w_st, st_x] + ([w_r12C] if qm > 0 else [])
        S.op("vector", "tensor_copy", c.r1, pb["A"][0:1, 0:512], waits=w)
        bgs(c, "A", S.ms("vector"))
        S.op("vector", "tensor_copy", c.r2, pb["B"][0:1, 0:512])
        w_xrt2 = S.ms("vector")
        bgs(c, "B", w_xrt2)
        w_q_used[qm] = w_lastS
        w_r12C = S.dma("sync", c.stats_h[0][None, qlo:qlo + MEGA], c.r1,
                       waits=[w_xrt2], ch="css")
        w_r12C = S.dma("sync", c.stats_h[1][None, qlo:qlo + MEGA], c.r2, ch="css")
    stats_to_mrs(c, w_r12C)
    if DBG_DUMP:
        dst4 = nc.dram_tensor("dbg_stats", [4, T], F32, kind="ExternalOutput").ap()
        S.dma("sync", dst4, c.stats_h, waits=[c.w_lnrows], ch="dbg")
        S.barrier()
    stC.close()


def emit_out(c, src="xres"):
    S, nc = c.S, c.nc
    pb = c.pbank
    with nc.sbuf_tensor("otm", [128, 2 * DM], F32) as otg, \
         nc.sbuf_tensor("ocm", [128, 2 * DM], F32) as ocg:
        otm, ocm = otg.ap(), ocg.ap()
        w_out = {}
        w_ev = {}
        for j in range(32):
            b = j % 2
            oc = ocm[:, b * DM:(b + 1) * DM]
            ot = otm[:, b * DM:(b + 1) * DM]
            w = [w_ev[j - 2]] if j >= 2 else []
            ld = S.dma("sync", oc,
                       c.xres.rearrange("c p t -> p c t")[:, :, j * 128:(j + 1) * 128],
                       waits=w, ch="eo%d" % b)
            for cc in range(NCH):
                bk = "A" if cc % 2 == 0 else "B"
                w2 = bgw(c, bk) + ([ld] if cc < 2 else [])
                S.op("tensor", "transpose", pb[bk][:, 0:128],
                     oc[:, cc * 128:(cc + 1) * 128], c.idf, waits=w2)
                wt = S.ms("tensor")
                w3 = [wt] + ([w_out[j - 2]] if (j >= 2 and cc == 0) else [])
                S.op("scalar", "copy", ot[:, cc * 128:(cc + 1) * 128], pb[bk][:, 0:128],
                     waits=w3)
                bgs(c, bk, S.ms("scalar"))
            w_ev[j] = S.now("tensor")
            w_sc = S.now("scalar")
            w_out[j] = S.dma("sync", c.y_out[j * 128:(j + 1) * 128, :], ot, waits=[w_sc], ch="es%d" % b)


def host_inputs(xb, W):
    d = {}
    d["x"] = np.ascontiguousarray(xb)
    win = W["L0_attn_a_in_proj_w"]
    win_p = np.zeros((NMCH, DM, 128), np.float32)
    for m in range(NMCH):
        cols = win[:, m * 128:min((m + 1) * 128, 4384)]
        win_p[m, :, :cols.shape[1]] = cols
    d["win_l"] = win_p
    d["wout"] = np.ascontiguousarray(W["L0_attn_a_out_proj_w"])
    cw = W["L0_attn_a_conv_w"]
    cwl = np.zeros((128, NXBC * 4), np.float32)
    for cc in range(NXBC):
        for k in range(4):
            cwl[:, cc * 4 + k] = cw[k, cc * 128:(cc + 1) * 128]
    d["conv_w"] = cwl
    d["conv_b"] = np.ascontiguousarray(W["L0_attn_a_conv_b"].reshape(NXBC, 128).T)
    d["norm_w"] = np.ascontiguousarray(W["L0_attn_a_norm_w"].reshape(16, 128).T)
    d["d_rep"] = np.ascontiguousarray(np.repeat(W["L0_attn_a_D"], HP).reshape(16, 128).T)
    d["dt_bias"] = np.ascontiguousarray(W["L0_attn_a_dt_bias"].reshape(32, 1))
    d["a_log"] = np.ascontiguousarray(W["L0_attn_a_A_log"].reshape(32, 1))
    wq = W["L0_attn_b_w_attn"]
    d["wqkv_l"] = np.ascontiguousarray(wq.reshape(DM, 24, 128).transpose(1, 0, 2))
    d["b_attn"] = np.ascontiguousarray(W["L0_attn_b_b_attn"].reshape(24, 128).T)
    d["wproj"] = np.ascontiguousarray(W["L0_attn_b_w_proj"])
    d["b_proj"] = np.ascontiguousarray(W["L0_attn_b_b_proj"].reshape(NCH, 128).T)
    d["wrT"] = np.ascontiguousarray(W["L0_ffn_W_r"].T)
    d["wkT"] = np.ascontiguousarray(W["L0_ffn_W_k"].T)
    d["wvT"] = np.ascontiguousarray(W["L0_ffn_W_v"].T)
    d["mu_r"] = np.ascontiguousarray(W["L0_ffn_mu_r"].reshape(NCH, 128).T)
    d["mu_k"] = np.ascontiguousarray(W["L0_ffn_mu_k"].reshape(NCH, 128).T)
    d["c_id"] = np.eye(128, dtype=np.float32)
    u = np.arange(128)
    d["c_tril"] = (u[:, None] <= u[None, :]).astype(np.float32)
    d["c_trius"] = (u[:, None] > u[None, :]).astype(np.float32)
    d["c_mask01"] = (u[None, :] >= u[:, None]).astype(np.float32)
    inv = 1.0 / 10000.0 ** (np.arange(0, HP, 2, dtype=np.float32) / HP)
    fr = np.arange(T, dtype=np.float32)[:, None] * inv[None, :]
    emb = np.repeat(fr, 2, axis=-1)
    d["c_cos"] = np.ascontiguousarray(np.tile(np.cos(emb).T, (2, 1)))
    d["c_sin"] = np.ascontiguousarray(np.tile(np.sin(emb).T, (2, 1)))
    perm = np.zeros((128, 128), np.float32)
    for blk in range(2):
        o = blk * 64
        for i in range(32):
            perm[o + 2 * i + 1, o + 2 * i] = -1.0
            perm[o + 2 * i, o + 2 * i + 1] = 1.0
    d["c_perm"] = perm
    d["c_ones"] = np.ones((128, 128), np.float32)
    cb = np.zeros((128, 2), np.float32)
    cb[:, 0] = EPS
    cb[:, 1] = -SHIFT
    d["c_bias"] = cb
    d["c_onec"] = np.ones((128, 1), np.float32)
    d["battn_r"] = np.ascontiguousarray(W["L0_attn_b_b_attn"].reshape(1, 3072))
    d["bproj_r"] = np.ascontiguousarray(W["L0_attn_b_b_proj"].reshape(1, 1024))
    return d


_prog_cache = {}


def kernel(**inputs):
    key = (DBG_PHASES, DBG_DUMP)
    if key not in _prog_cache:
        _prog_cache[key] = build_core_program()
    nc = _prog_cache[key]
    x = inputs["L0_attn_a_x"]
    in_maps = [host_inputs(x[b], inputs) for b in range(8)]
    res = run_bass_kernel_spmd(nc, in_maps, core_ids=list(range(8)))
    return np.stack([res.results[b]["y"] for b in range(8)], axis=0)


# version 22
# speedup vs baseline: 1.0193x; 1.0193x over previous
import numpy as np
import sys
from contextlib import ExitStack

sys.path.insert(0, "/opt/trn_rl_repo")
import concourse.bass as bass
import concourse.mybir as mybir
from concourse.bass_utils import run_bass_kernel_spmd

F32 = mybir.dt.float32
BF16 = mybir.dt.bfloat16
AF = mybir.ActivationFunctionType
ALU = mybir.AluOpType
AXX = mybir.AxisListType.X

T = 4096
DM = 1024
NCH = 8
D_INNER = 2048
NH = 32
HP = 64
NXBC = 18
NMCH = 35
MEGA = 512
NMEGA = T // MEGA
EPS = 1e-5
SHIFT = 50.0

DBG_PHASES = "all"
DBG_DUMP = False


def fbc(ap, n=128):
    return bass.AP(ap.tensor, ap.offset, [list(ap.ap[0]), [0, n]])


def v3(ap, rep, n=128):
    return bass.AP(ap.tensor, ap.offset, [list(ap.ap[0]), [n, rep], [1, n]])


def rep_mid(ap, rep, n=128):
    return bass.AP(ap.tensor, ap.offset, [list(ap.ap[0]), [0, rep], [1, n]])


def colrep(ap, col0, nh, rep=64):
    return bass.AP(ap.tensor, ap.offset + col0,
                   [list(ap.ap[0]), [1, nh], [0, rep]])


def kchunk_view(ap2, nk=8, nm=128):
    return bass.AP(ap2.tensor, ap2.offset, [[nm, 128], [128 * nm, nk], [1, nm]])


def tile_view(dram_ap, tlo, nt, nch=NCH):
    return bass.AP(dram_ap.tensor, dram_ap.offset + tlo,
                   [[T, 128], [128 * T, nch], [1, nt]])


class Sched:
    ENGINES = ("tensor", "vector", "scalar", "gpsimd", "sync")

    def __init__(self, nc):
        self.nc = nc
        self.q = {e: [] for e in self.ENGINES}
        self.sems = {}
        self.cnt = {}
        self.last = {e: None for e in self.ENGINES}
        self.esem = {e: self.mksem("ms_" + e) for e in self.ENGINES}
        self.qch = set()
        self.dirty = {e: set() for e in self.ENGINES}
        self.rdset = {e: set() for e in self.ENGINES}

    def mksem(self, name):
        h = self.nc.semaphore(name).__enter__()
        self.sems[name] = h
        self.cnt[name] = 0
        return name

    def wait(self, eng, sem, val):
        if val > 0:
            self.q[eng].append(["wait", sem, val])

    def waits(self, eng, ws):
        seen = {}
        for w in ws:
            if w is None:
                continue
            s, v = w
            seen[s] = max(seen.get(s, 0), v)
        for s, v in seen.items():
            self.wait(eng, s, v)

    @staticmethod
    def _ap_names(args, kwargs):
        names = []
        for a in args:
            if isinstance(a, bass.AP):
                names.append(a.tensor.name)
        for a in kwargs.values():
            if isinstance(a, bass.AP):
                names.append(a.tensor.name)
        return names

    def op(self, eng, method, *args, waits=(), **kwargs):
        if eng == "tensor":
            self.waits(eng, waits)
            ent = ["ins", method, args, kwargs, []]
            self.q[eng].append(ent)
            self.last[eng] = ent
            return ent
        names = self._ap_names(args, kwargs)
        out_n = names[0] if names else None
        reads = set(names[1:])
        need = (bool(reads & self.dirty[eng])
                or (out_n in self.dirty[eng])
                or (out_n in self.rdset[eng]))
        if need and self.last[eng] is not None:
            w = None
            ent = self.last[eng]
            for (sem, by) in ent[4]:
                if sem == self.esem[eng]:
                    w = (sem, self.cnt[sem])
            if w is None:
                if ent[1] == "dma_start":
                    w = None
                else:
                    w = self.ms(eng)
            if w is not None:
                self.wait(eng, w[0], w[1])
            self.dirty[eng].clear()
            self.rdset[eng].clear()
        self.waits(eng, waits)
        ent = ["ins", method, args, kwargs, []]
        self.q[eng].append(ent)
        self.last[eng] = ent
        if out_n:
            self.dirty[eng].add(out_n)
        self.rdset[eng] |= reads
        return ent

    def ms(self, eng, sem=None, by=1):
        sem = sem or self.esem[eng]
        ent = self.last[eng]
        assert ent is not None
        self.cnt[sem] += by
        ent[4].append((sem, by))
        return (sem, self.cnt[sem])

    def now(self, eng):
        return (self.esem[eng], self.cnt[self.esem[eng]])

    def chain(self, eng):
        w = self.ms(eng)
        self.wait(eng, w[0], w[1])
        return w

    def dma(self, queue, dst, src, waits=(), ch="misc"):
        sem = "q_" + ch
        if sem not in self.sems:
            self.mksem(sem)
            self.qch.add(sem)
        self.waits(queue, waits)
        ent = ["ins", "dma_start", (dst, src), {}, []]
        self.q[queue].append(ent)
        self.last[queue] = ent
        self.cnt[sem] += 16
        ent[4].append((sem, 16))
        return (sem, self.cnt[sem])

    def barrier(self):
        vals = {}
        for e in self.ENGINES:
            ent = self.last[e]
            if ent is not None and ent[1] != "dma_start" and not ent[4]:
                vals[e] = self.ms(e)
            else:
                vals[e] = self.now(e)
        qvals = [(sem, self.cnt[sem]) for sem in sorted(self.qch)]
        for e in self.ENGINES:
            for o in self.ENGINES:
                if o != e:
                    self.wait(e, vals[o][0], vals[o][1])
            for sem, v in qvals:
                self.wait(e, sem, v)

    def replay(self, eng, h):
        for ent in self.q[eng]:
            if ent[0] == "wait":
                h.wait_ge(self.sems[ent[1]], ent[2])
            else:
                ins = getattr(h, ent[1])(*ent[2], **ent[3])
                for (sem, by) in ent[4]:
                    ins.then_inc(self.sems[sem], by)


class Ctx:
    pass


def bgw(c, *names):
    return [c.bg[n] for n in names if n in c.bg]


def bgs(c, name, w):
    c.bg[name] = w


def build_core_program():
    nc = bass.Bass("TRN2", target_bir_lowering=False, debug=False)
    c = Ctx()
    c.nc = nc
    c.bg = {}

    def din(name, shape):
        return nc.dram_tensor(name, shape, F32, kind="ExternalInput").ap()

    c.x_in = din("x", [T, DM])
    c.win_l = din("win_l", [NMCH, DM, 128])
    c.wout = din("wout", [D_INNER, DM])
    c.conv_w = din("conv_w", [128, NXBC * 4])
    c.conv_b = din("conv_b", [128, NXBC])
    c.norm_w = din("norm_w", [128, 16])
    c.d_rep = din("d_rep", [128, 16])
    c.dt_bias = din("dt_bias", [32, 1])
    c.a_log = din("a_log", [32, 1])
    c.wqkv_l = din("wqkv_l", [24, DM, 128])
    c.b_attn = din("b_attn", [128, 24])
    c.wproj = din("wproj", [DM, DM])
    c.b_proj = din("b_proj", [128, NCH])
    c.wrT = din("wrT", [DM, DM])
    c.wkT = din("wkT", [DM, DM])
    c.wvT = din("wvT", [DM, DM])
    c.mu_r = din("mu_r", [128, NCH])
    c.mu_k = din("mu_k", [128, NCH])
    c.c_id = din("c_id", [128, 128])
    c.c_tril = din("c_tril", [128, 128])
    c.c_trius = din("c_trius", [128, 128])
    c.c_mask01 = din("c_mask01", [128, 128])
    c.c_cos = din("c_cos", [128, T])
    c.c_sin = din("c_sin", [128, T])
    c.c_perm = din("c_perm", [128, 128])
    c.c_ones = din("c_ones", [128, 128])
    c.c_bias = din("c_bias", [128, 2])
    c.c_onec = din("c_onec", [128, 1])
    c.battn_r = din("battn_r", [1, 24 * 128])
    c.bproj_r = din("bproj_r", [1, NCH * 128])

    c.y_out = nc.dram_tensor("y", [T, DM], F32, kind="ExternalOutput").ap()
    c.xres = nc.dram_tensor("xres", [NCH, 128, T], F32, kind="Internal").ap()
    c.xn_h = nc.dram_tensor("xn_h", [NCH, 128, T], BF16, kind="Internal").ap()
    c.winb_h = nc.dram_tensor("winb_h", [NMCH, 128, NCH * 128], BF16, kind="Internal").ap()
    c.q_hbm = nc.dram_tensor("q_hbm", [NCH, 128, T], BF16, kind="Internal").ap()
    c.den_h = nc.dram_tensor("den_h", [16, MEGA], F32, kind="Internal").ap()
    c.stats_h = nc.dram_tensor("stats_h", [4, T], F32, kind="Internal").ap()
    c.dta_h = nc.dram_tensor("dta_h", [2, 32, T], F32, kind="Internal").ap()
    c.denb_h = nc.dram_tensor("denb_h", [1, 16 * MEGA], BF16, kind="Internal").ap()

    S = Sched(nc)
    c.S = S

    al = nc.alloc_sbuf_tensor
    for nm, sh, dt in [
            ("idf", [128, 128], F32), ("idb", [128, 128], BF16),
            ("tril", [128, 128], F32), ("trius", [128, 128], F32),
            ("mask01b", [128, 128], BF16),
            ("onescf", [128, 1], F32), ("onescb", [128, 1], BF16),
            ("onesrf", [1, 128], F32), ("onesrb", [1, 128], BF16),
            ("convw_sb", [128, NXBC * 4], F32), ("convb_sb", [128, NXBC], F32),
            ("normw_sb", [128, 16], F32), ("drep_sb", [128, 16], F32),
            ("dtb_sb", [32, 1], F32), ("alog_sb", [32, 1], F32),
            ("A_sb", [32, 1], F32),
            ("mur_sb", [128, NCH], F32), ("muk_sb", [128, NCH], F32),
            ("battn_sb", [128, 24], F32), ("bproj_sb", [128, NCH], F32),
            ("ebias", [128, 2], F32),
            ("r1", [1, 512], F32), ("r2", [1, 512], F32), ("r3", [1, 512], F32),
            ]:
        setattr(c, nm, al(nm, sh, dt).ap())

    c.pbank = {}
    for nm in ["A", "B"]:
        c.pbank[nm] = nc.alloc_psum_tensor("p" + nm, [128, 512], F32).ap()
    for pnm in ["CD", "EF"]:
        pp = nc.alloc_psum_tensor("p" + pnm, [128, 1024], F32).ap()
        c.pbank[pnm] = pp
        c.pbank[pnm[0]] = pp[:, 0:512]
        c.pbank[pnm[1]] = pp[:, 512:1024]
    c.pbank["T1"] = nc.alloc_psum_tensor("pT1", [128, 1024], BF16).ap()
    c.pbank["T2"] = nc.alloc_psum_tensor("pT2", [128, 1024], BF16).ap()

    phase0(c)
    S.barrier()
    c.bg = {}
    if DBG_DUMP:
        dxn = nc.dram_tensor("dbg_xn", [NCH, 128, T], BF16, kind="ExternalOutput").ap()
        ddta = nc.dram_tensor("dbg_dta", [2, 32, T], F32, kind="ExternalOutput").ap()
        dxr = nc.dram_tensor("dbg_xres0", [NCH, 128, T], F32, kind="ExternalOutput").ap()
        S.dma("sync", dxn, c.xn_h, ch="dbg")
        S.dma("sync", ddta, c.dta_h, ch="dbg")
        S.dma("sync", dxr, c.xres, ch="dbg")
        S.barrier()
    phaseA(c)
    S.barrier()
    c.bg = {}
    if DBG_PHASES == "A":
        emit_out(c, src="xres")
    else:
        phaseB(c, tag=1)
        S.barrier()
        c.bg = {}
        if DBG_PHASES == "B1":
            emit_out(c, src="xres")
        else:
            phaseC(c)
            S.barrier()
            c.bg = {}
            if DBG_PHASES == "C":
                emit_out(c, src="xres")
            else:
                phaseB(c, tag=2)

    for sem in sorted(S.qch):
        S.wait("sync", sem, S.cnt[sem])
        S.wait("gpsimd", sem, S.cnt[sem])

    with nc.Block() as block:
        @block.tensor
        def _(e):
            S.replay("tensor", e)

        @block.vector
        def _(e):
            S.replay("vector", e)

        @block.scalar
        def _(e):
            S.replay("scalar", e)

        @block.gpsimd
        def _(e):
            S.replay("gpsimd", e)

        @block.sync
        def _(e):
            S.replay("sync", e)

    return nc


def phase0(c):
    S, nc = c.S, c.nc
    pb = c.pbank
    last = None
    for dst, src in [(c.idf, c.c_id), (c.tril, c.c_tril), (c.trius, c.c_trius),
                     (c.onescf, c.c_onec), (c.onesrf, c.c_ones[0:1, :]),
                     (c.convw_sb, c.conv_w), (c.convb_sb, c.conv_b),
                     (c.normw_sb, c.norm_w), (c.drep_sb, c.d_rep),
                     (c.dtb_sb, c.dt_bias), (c.alog_sb, c.a_log),
                     (c.mur_sb, c.mu_r), (c.muk_sb, c.mu_k),
                     (c.battn_sb, c.b_attn), (c.bproj_sb, c.b_proj),
                     (c.ebias, c.c_bias)]:
        w_c = S.dma("sync", dst, src, ch="c0")
    for dst, src in [(c.idb, c.c_id), (c.mask01b, c.c_mask01),
                     (c.onescb, c.c_onec), (c.onesrb, c.c_ones[0:1, :])]:
        w_g = S.dma("gpsimd", dst, src, ch="g0")

    for e in Sched.ENGINES:
        S.wait(e, w_c[0], w_c[1])
        S.wait(e, w_g[0], w_g[1])
    S.op("scalar", "activation", c.A_sb, c.alog_sb, AF.Exp, waits=[w_c])
    w_sA = S.ms("scalar")
    S.op("vector", "tensor_scalar", c.A_sb, c.A_sb, -1.0, None, ALU.mult, waits=[w_sA])
    c.w_A = S.ms("vector")

    c.stW = ExitStack()
    c.Sst = c.stW.enter_context(nc.sbuf_tensor("Sst", [128, NH * HP], F32)).ap()
    c.Sstb = c.stW.enter_context(nc.sbuf_tensor("Sstb", [128, NH * HP], BF16)).ap()
    c.wdt_sb = c.stW.enter_context(nc.sbuf_tensor("wdt_sb", [128, NCH * 128], BF16)).ap()
    w_wdt = S.dma("gpsimd", c.wdt_sb, kchunk_view(c.win_l[34]), ch="gw")
    st = {}
    with nc.sbuf_tensor("wtmp", [128, 2 * NCH * 128], BF16) as wtg:
        wtmp = wtg.ap()
        for m in range(NMCH):
            b = m % 2
            sl = wtmp[:, b * 1024:(b + 1) * 1024]
            src = kchunk_view(c.win_l[m])
            w = [st[m - 2]] if m >= 2 else []
            ldw = S.dma("gpsimd", sl, src, waits=w, ch="wr%d" % b)
            st[m] = S.dma("gpsimd", c.winb_h[m], sl, waits=[ldw], ch="ws%d" % b)
    c.woutb = c.stW.enter_context(nc.sbuf_tensor("woutb", [128, 16 * 1024], BF16)).ap()
    for kc in range(16):
        w = [st[NMCH - 1], st[NMCH - 2]] if kc < 2 else []
        S.dma("gpsimd", c.woutb[:, kc * 1024:(kc + 1) * 1024],
              c.wout[kc * 128:(kc + 1) * 128, :], ch="g%d" % (kc % 2), waits=w)

    st0 = ExitStack()
    dt_sb = st0.enter_context(nc.sbuf_tensor("dt_sb", [32, T], F32)).ap()
    dtr_sb = st0.enter_context(nc.sbuf_tensor("dtr_sb", [32, T], F32)).ap()
    a_sb = st0.enter_context(nc.sbuf_tensor("a_sb", [32, T], F32)).ap()
    xtm = st0.enter_context(nc.sbuf_tensor("xtm", [128, 4 * DM], F32)).ap()
    x2t = st0.enter_context(nc.sbuf_tensor("x2t", [128, DM], F32)).ap()
    xnt = st0.enter_context(nc.sbuf_tensor("xnt0", [128, 4 * DM], BF16)).ap()
    xcs = st0.enter_context(nc.sbuf_tensor("xcs", [128, 4 * DM], F32)).ap()
    xns = st0.enter_context(nc.sbuf_tensor("xns", [128, 4 * DM], BF16)).ap()
    statb = st0.enter_context(nc.sbuf_tensor("stat", [128, 32], F32)).ap()

    w_xstore = {}
    w_nstore = {}
    w_tdone = {}
    w_xnapply = {}
    w_dtev = {}
    w_xnsdone = {}
    for j in range(32):
        b = j % 4
        xt = xtm[:, b * DM:(b + 1) * DM]
        xn_t = xnt[:, b * DM:(b + 1) * DM]
        xc_s = xcs[:, b * DM:(b + 1) * DM]
        xn_s = xns[:, b * DM:(b + 1) * DM]
        stat = statb[:, b * 8:(b + 1) * 8]
        w = []
        if j >= 4:
            w = [w_tdone[j - 4], w_xnapply[j - 4]]
        ld = S.dma("sync", xt, c.x_in[j * 128:(j + 1) * 128, :], waits=w, ch="xt%d" % b)
        S.op("vector", "reduce_sum", stat[:, 0:1], xt, axis=AXX, waits=[ld])
        S.op("vector", "tensor_mul", x2t, xt, xt)
        S.op("vector", "reduce_sum", stat[:, 1:2], x2t, axis=AXX)
        S.op("vector", "tensor_scalar", stat[:, 2:3], stat[:, 0:1], 1.0 / DM, None, ALU.mult)
        S.chain("vector")
        S.op("vector", "tensor_mul", stat[:, 3:4], stat[:, 2:3], stat[:, 2:3])
        S.chain("vector")
        S.op("vector", "scalar_tensor_tensor", stat[:, 3:4], stat[:, 1:2], 1.0 / DM,
             stat[:, 3:4], ALU.mult, ALU.subtract)
        w_var = S.ms("vector")
        S.op("scalar", "activation", stat[:, 4:5], stat[:, 3:4], AF.Ln,
             bias=c.ebias[:, 0:1], scale=1.0, waits=[w_var])
        S.chain("scalar")
        S.op("scalar", "activation", stat[:, 4:5], stat[:, 4:5], AF.Exp,
             bias=0.0, scale=-0.5)
        w_rstd = S.ms("scalar")
        S.op("vector", "tensor_scalar", xn_t, xt, stat[:, 2:3], stat[:, 4:5],
             ALU.subtract, ALU.mult, waits=[w_rstd])
        w_xnapply[j] = S.ms("vector")
        bka, bkb = ("A", "B") if j % 2 == 0 else ("E", "F")
        bkt1 = "T1" if j % 2 == 0 else "T2"
        bkc = "C" if j % 2 == 0 else "D"
        for cc in range(NCH):
            bk = bka if cc < 4 else bkb
            w = (bgw(c, bk) + [ld]) if cc % 4 == 0 else []
            S.op("tensor", "transpose", pb[bk][:, (cc % 4) * 128:(cc % 4) * 128 + 128],
                 xt[:, cc * 128:(cc + 1) * 128], c.idf, waits=w)
            if cc % 4 == 3:
                w_tp = S.ms("tensor")
                w2 = [w_tp] + ([w_xstore[j - 4]] if (j >= 4 and cc == 3) else [])
                S.op("scalar", "copy", xc_s[:, (cc - 3) * 128:(cc + 1) * 128],
                     pb[bk][:, 0:512], waits=w2)
                bgs(c, bk, S.ms("scalar"))
        w_xcs = S.now("scalar")
        for cc in range(NCH):
            w = bgw(c, bkt1) + [w_xnapply[j]] if cc == 0 else []
            if j == 0 and cc == 0:
                w.append(w_g)
            S.op("tensor", "transpose", pb[bkt1][:, cc * 128:(cc + 1) * 128],
                 xn_t[:, cc * 128:(cc + 1) * 128], c.idb, waits=w)
        w_tp = S.ms("tensor")
        w2 = [w_tp]
        if j >= 4:
            w2 += [w_nstore[j - 4], w_xnsdone[j - 4]]
        S.op("vector", "tensor_copy", xn_s, pb[bkt1][:, 0:1024], waits=w2)
        bgs(c, bkt1, S.ms("vector"))
        w_xns = S.now("vector")
        w = [w_wdt, w_xns] + bgw(c, bkc)
        for kc in range(NCH):
            S.op("tensor", "matmul", pb[bkc][0:32, 0:128],
                 c.wdt_sb[:, kc * 128:kc * 128 + 32], xn_s[:, kc * 128:(kc + 1) * 128],
                 start=(kc == 0), stop=(kc == 7), waits=w if kc == 0 else [])
        w_mm = S.ms("tensor")
        w_tdone[j] = S.now("tensor")
        w_xnsdone[j] = w_mm
        S.op("vector", "tensor_copy", dtr_sb[:, j * 128:(j + 1) * 128],
             pb[bkc][0:32, 0:128], waits=[w_mm])
        bgs(c, bkc, S.ms("vector"))
        w_xstore[j] = S.dma("sync",
                            c.xres.rearrange("c p t -> p c t")[:, :, j * 128:(j + 1) * 128],
                            xc_s, waits=[w_xcs], ch="xs%d" % b)
        w_nstore[j] = S.dma("sync",
                            c.xn_h.rearrange("c p t -> p c t")[:, :, j * 128:(j + 1) * 128],
                            xn_s, waits=[w_xns], ch="ns%d" % b)

    S.op("vector", "tensor_scalar", dt_sb, dtr_sb, c.dtb_sb[:, 0:1], None,
         ALU.add)
    w_dtb = S.chain("vector")
    S.op("scalar", "activation", dtr_sb, dt_sb, AF.Abs, waits=[w_dtb])
    S.op("scalar", "activation", dtr_sb, dtr_sb, AF.Exp, bias=0.0, scale=-1.0)
    S.op("scalar", "activation", dtr_sb, dtr_sb, AF.Ln, bias=1.0, scale=1.0)
    w_sp = S.ms("scalar")
    S.op("vector", "tensor_scalar", a_sb, dt_sb, 0.0, None, ALU.max)
    S.chain("vector")
    S.op("vector", "tensor_add", dt_sb, dtr_sb, a_sb, waits=[w_sp])
    S.chain("vector")
    S.op("vector", "tensor_scalar", a_sb, dt_sb, c.A_sb[:, 0:1], None, ALU.mult,
         waits=[c.w_A])
    w_dta = S.ms("vector")
    S.dma("sync", c.dta_h[0], dt_sb, waits=[w_dta], ch="dta")
    S.dma("sync", c.dta_h[1], a_sb, waits=[w_dta], ch="dta")
    S.op("vector", "memset", c.Sst, 0.0)
    S.op("vector", "memset", c.Sstb, 0.0)
    S.ms("vector")
    st0.close()


def phaseA(c):
    S, nc = c.S, c.nc
    pb = c.pbank
    stA = ExitStack()

    def al(nm, sh, dt):
        return stA.enter_context(nc.sbuf_tensor(nm, sh, dt))

    winm = al("winm", [128, 3 * NCH * 128], BF16).ap()
    xnb = al("xnb", [128, 2 * NCH * 512], BF16).ap()
    sz = al("sz", [128, 16 * 512], BF16).ap()
    xbc = al("xbc", [128, NXBC * 515], BF16).ap()
    xbca = bass.AP(xbc.tensor, xbc.offset + 3,
                   [list(xbc.ap[0]), [515, NXBC], [1, 512]])
    y2 = al("y2", [128, 16 * 512], BF16).ap()
    xrt = al("xrt", [128, NCH * 512], F32).ap()
    dw = al("dw", [128, 4 * 512], BF16).ap()
    t8w = al("t8w", [128, 1024], BF16).ap()
    Uall = al("Uall", [128, NH * HP], BF16).ap()
    Xtall = al("Xtall", [128, NH * HP], BF16).ap()
    aT = al("aT", [128, 128], F32).ap()
    cumcT = al("cumcT", [128, 128], F32).ap()
    revT = al("revT", [128, 128], F32).ap()
    EcumT = al("EcumT", [128, 128], F32).ap()
    sc2T = al("sc2T", [128, 128], F32).ap()
    dtT = al("dtT", [128, 128], F32).ap()
    AtotB = al("AtotB", [128, 128], F32).ap()
    exl4 = al("exl4", [1, 128], F32).ap()
    Gmsk = al("Gmsk", [128, 128], BF16).ap()
    BT = al("BT", [128, 128], BF16).ap()
    D4 = al("D4", [128, 1024], F32).ap()
    E4 = al("E4", [128, 1024], F32).ap()
    Gm4 = al("Gm4", [128, 2 * 1024], BF16).ap()
    tYb = al("tYb", [128, NH * HP], BF16).ap()
    ypra = al("ypra", [128, NH * HP], BF16).ap()
    g2t = al("g2t", [128, 2 * 512], BF16).ap()
    tmpS = al("tmpS", [128, 512], F32).ap()
    rsr = al("rsr", [1, 512], F32).ap()
    halo = al("halo", [128, NXBC * 3], BF16).ap()
    dtm = al("dtm", [32, 512], F32).ap()
    am = al("am", [32, 512], F32).ap()

    S.op("vector", "memset", halo, 0.0)
    w_init = S.ms("vector")

    def xav(cc, qsl2=None):
        off = xbc.offset + cc * 515 + 3
        if qsl2 is not None:
            off += qsl2.start
            return bass.AP(xbc.tensor, off, [list(xbc.ap[0]), [1, qsl2.stop - qsl2.start]])
        return bass.AP(xbc.tensor, off, [list(xbc.ap[0]), [1, 512]])

    Bsl = xav(16)
    Csl = xav(17)

    w_xn_used = {}
    w_win_used = {}
    w_zgate = None
    w_xrt_done = None
    w_e4 = None
    w_gm4 = None
    w_uxread = None
    w_yc_last = None
    w_yprtp_last = None
    w_e4 = None
    w_dta_used = None

    for mg in range(NMEGA):
        tlo = mg * MEGA
        bb = mg % 2
        xnt = xnb[:, bb * NCH * 512:(bb + 1) * NCH * 512]
        w = [w_xn_used[mg - 2]] if mg >= 2 else []
        ld_xn = S.dma("sync", xnt, tile_view(c.xn_h, tlo, MEGA), waits=w, ch="an%d" % bb)
        w = [w_xrt_done] if w_xrt_done else []
        ld_xr = S.dma("sync", xrt, tile_view(c.xres, tlo, MEGA), waits=w, ch="ar")
        w = [w_dta_used] if mg > 0 else []
        S.dma("sync", dtm, c.dta_h[0][:, tlo:tlo + MEGA], waits=w, ch="dl")
        ld_dta = S.dma("sync", am, c.dta_h[1][:, tlo:tlo + MEGA], ch="dl")

        w_ipz = {}
        w_ipx = {}
        w_conv = {}
        w_dwmm = {}
        pend_conv = []
        ci_conv = [0]
        m_order = [32, 33] + list(range(16, 32)) + list(range(16))
        for mi, m in enumerate(m_order):
            wb = mi % 3
            wsl = winm[:, wb * NCH * 128:(wb + 1) * NCH * 128]
            gm = mg * 34 + mi
            w = [w_win_used[gm - 3]] if gm >= 3 else []
            ld_w = S.dma("sync", wsl, c.winb_h[m], waits=w, ch="aw%d" % wb)
            bk = "ABEF"[mi % 4]
            w = bgw(c, bk) + [ld_w, ld_xn]
            if mg == 0 and mi == 0:
                w.append(w_init)
            if w_zgate is not None and mi == 0:
                w.append(w_zgate)
            for kc in range(NCH):
                S.op("tensor", "matmul", pb[bk][:, 0:512],
                     wsl[:, kc * 128:(kc + 1) * 128], xnt[:, kc * 512:(kc + 1) * 512],
                     start=(kc == 0), stop=(kc == 7), waits=w if kc == 0 else [])
            w_mm = S.ms("tensor")
            w_win_used[gm] = w_mm
            if mi == len(m_order) - 1:
                w_xn_used[mg] = w_mm
            if m < 16:
                S.op("scalar", "activation", sz[:, m * 512:(m + 1) * 512],
                     pb[bk][:, 0:512], AF.Silu, waits=[w_mm])
                w_ipz[m] = S.ms("scalar")
                bgs(c, bk, w_ipz[m])
            else:
                cc = m - 16
                S.op("vector", "tensor_copy", xbc[:, cc * 515 + 3:cc * 515 + 515],
                     pb[bk][:, 0:512], waits=[w_mm])
                w_ipx[cc] = S.ms("vector")
                bgs(c, bk, w_ipx[cc])
            def drain_conv(keep):
                while len(pend_conv) > keep:
                    cc2p, basep, dslotp, dslp, (w_dwp, w_halop) = pend_conv.pop(0)
                    w = bgw(c, "D") + [w_dwp, w_halop]
                    for k in range(4):
                        S.op("tensor", "matmul", pb["D"][:, 0:512],
                             dslp[:, k * 128:(k + 1) * 128],
                             xbc[:, basep + k:basep + k + 512],
                             start=(k == 0), stop=(k == 3), waits=w if k == 0 else [])
                    w_dwmm[dslotp] = S.ms("tensor")
                    S.op("scalar", "activation",
                         bass.AP(xbc.tensor, xbc.offset + cc2p * 515 + 3,
                                 [list(xbc.ap[0]), [1, 512]]), pb["D"][:, 0:512],
                         AF.Silu, bias=c.convb_sb[:, cc2p:cc2p + 1],
                         waits=[w_dwmm[dslotp]])
                    w_conv[cc2p] = S.ms("scalar")
                    bgs(c, "D", w_conv[cc2p])
            drain_conv(2)
            if m >= 16:
                cc = m - 16
                base = cc * 515
                dslot = ci_conv[0] % 4
                ci_conv[0] += 1
                dsl2 = dw[:, dslot * 512:dslot * 512 + 512]
                S.op("vector", "tensor_copy", xbc[:, base:base + 3],
                     halo[:, cc * 3:(cc + 1) * 3], waits=[w_ipx[cc]])
                S.chain("vector")
                S.op("vector", "tensor_copy", halo[:, cc * 3:(cc + 1) * 3],
                     xbc[:, base + 512:base + 515])
                w_halo = S.ms("vector")
                w = [w_dwmm[dslot]] if dslot in w_dwmm else []
                for k in range(4):
                    S.op("vector", "tensor_scalar", dsl2[:, k * 128:(k + 1) * 128],
                         c.idb, c.convw_sb[:, cc * 4 + 3 - k:cc * 4 + 4 - k], None,
                         ALU.mult, waits=w if k == 0 else [])
                w_dw = S.ms("vector")
                pend_conv.append((cc, base, dslot, dsl2, (w_dw, w_halo)))
            if mi == len(m_order) - 1:
                drain_conv(0)

        if DBG_DUMP and mg == 0:
            dxbca = nc.dram_tensor("dbg_xbca", [128, NXBC * 512], BF16,
                                   kind="ExternalOutput").ap()
            dsz = nc.dram_tensor("dbg_sz", [128, 16 * 512], BF16,
                                 kind="ExternalOutput").ap()
            S.dma("sync", dxbca, bass.AP(xbc.tensor, xbc.offset + 3, [list(xbc.ap[0]), [515, NXBC], [1, 512]]), waits=[w_conv[cc2] for cc2 in range(NXBC)], ch="dbg")
            S.dma("sync", dsz, sz, waits=[w_ipz[mm2] for mm2 in range(16)], ch="dbg")
            S.barrier()
        w_y2 = {}
        w_lasttY = None
        w = bgw(c, "C") + [ld_dta]
        for qq2 in range(4):
            S.op("tensor", "transpose", pb["C"][:, qq2 * 32:(qq2 + 1) * 32],
                 am[:, qq2 * 128:(qq2 + 1) * 128], c.idf[0:32, 0:32],
                 waits=w if qq2 == 0 else [])
        w1 = S.ms("tensor")
        S.op("vector", "tensor_copy", aT, pb["C"][:, 0:128], waits=[w1])
        wv = S.ms("vector")
        S.op("tensor", "matmul", pb["C"][:, 128:256], c.tril, aT,
             start=True, stop=True, waits=[wv])
        S.op("tensor", "matmul", pb["C"][:, 256:384], c.trius, aT,
             start=True, stop=True)
        S.op("tensor", "matmul", pb["C"][0:1, 384:512], c.onescf, aT,
             start=True, stop=True)
        w2a = S.ms("tensor")
        w = bgw(c, "D")
        for qq2 in range(4):
            S.op("tensor", "transpose", pb["D"][:, qq2 * 32:(qq2 + 1) * 32],
                 dtm[:, qq2 * 128:(qq2 + 1) * 128], c.idf[0:32, 0:32],
                 waits=w if qq2 == 0 else [])
        w2 = S.ms("tensor")
        S.op("vector", "tensor_copy", cumcT, pb["C"][:, 128:256], waits=[w2a])
        S.op("vector", "tensor_copy", revT, pb["C"][:, 256:384])
        S.op("vector", "tensor_copy", dtT, pb["D"][:, 0:128], waits=[w2])
        wv = S.ms("vector")
        S.op("scalar", "activation", EcumT, cumcT, AF.Exp, waits=[wv])
        S.op("scalar", "activation", sc2T, revT, AF.Exp)
        ws = S.ms("scalar")
        S.op("vector", "tensor_mul", sc2T, sc2T, dtT, waits=[ws])
        w_sc2 = S.ms("vector")
        S.op("scalar", "activation", exl4, pb["C"][0:1, 384:512],
             AF.Exp, waits=[wv])
        w_exl = S.ms("scalar")
        S.op("tensor", "matmul", pb["D"][:, 128:256], c.onesrf, exl4,
             start=True, stop=True, waits=[w_exl])
        w3 = S.ms("tensor")
        S.op("vector", "tensor_copy", AtotB, pb["D"][:, 128:256], waits=[w3])
        w_prelim0 = S.ms("vector")
        bgs(c, "C", w_prelim0)
        bgs(c, "D", w_prelim0)
        if True:
            w_dta_used = w2
        for qq in range(4):
            ci = mg * 4 + qq
            hb = qq * 32
            qsl = slice(qq * 128, (qq + 1) * 128)
            S.op("vector", "tensor_copy", c.Sstb, c.Sst,
                 waits=[w_prelim0] if qq == 0 else [])
            w_prelim = S.ms("vector")
            w = bgw(c, "D") + [w_prelim, w_conv[16], w_conv[17]]
            S.op("tensor", "matmul", pb["D"][:, 0:128], xav(16, qsl), xav(17, qsl),
                 start=True, stop=True, waits=w)
            w4 = S.ms("tensor")
            S.op("vector", "tensor_mul", Gmsk, pb["D"][:, 0:128], c.mask01b, waits=[w4])
            bgs(c, "D", S.ms("vector"))
            w = bgw(c, "T1")
            S.op("tensor", "transpose", pb["T1"][:, 0:128], xav(16, qsl), c.idb, waits=w)
            w5 = S.ms("tensor")
            S.op("vector", "tensor_copy", BT, pb["T1"][:, 0:128], waits=[w5])
            bgs(c, "T1", S.ms("vector"))
            for bi, bkt in enumerate(("T1", "T2")):
                for sl8 in range(8):
                    c2 = bi * 8 + sl8
                    w = bgw(c, bkt) + [w_conv[c2]] if sl8 == 0 else [w_conv[c2]]
                    if w_uxread is not None and c2 == 0:
                        w.append(w_uxread)
                    S.op("tensor", "transpose", pb[bkt][:, sl8 * 128:(sl8 + 1) * 128],
                         xav(c2, qsl), c.idb, waits=w)
                w6 = S.ms("tensor")
                hb2 = bi * 16
                S.op("vector", "tensor_mul",
                     v3(Uall[:, hb2 * 64:(hb2 + 16) * 64], 16, 64),
                     v3(pb[bkt][:, 0:1024], 16, 64),
                     colrep(dtT, hb + hb2, 16), waits=[w6])
                S.op("vector", "tensor_mul",
                     v3(Xtall[:, hb2 * 64:(hb2 + 16) * 64], 16, 64),
                     v3(pb[bkt][:, 0:1024], 16, 64),
                     colrep(sc2T, hb + hb2, 16))
                bgs(c, bkt, S.ms("vector"))
            w_ux = S.now("vector")
            w_tY8 = {}
            for h8 in range(4):
                bkc = "C" if h8 % 2 == 0 else "D"
                for hh in range(8):
                    h = h8 * 8 + hh
                    w = bgw(c, bkc) if hh == 0 else []
                    if h == 0:
                        w = w + [w_prelim]
                        if w_yc_last is not None:
                            w.append(w_yc_last)
                    S.op("tensor", "matmul", pb[bkc][:, hh * 64:(hh + 1) * 64],
                         xav(17, qsl), c.Sstb[:, h * 64:(h + 1) * 64],
                         start=True, stop=True, waits=w)
                w8 = S.ms("tensor")
                S.op("vector", "tensor_mul",
                     v3(tYb[:, h8 * 512:(h8 + 1) * 512], 8, 64),
                     v3(pb[bkc][:, 0:512], 8, 64),
                     colrep(EcumT, hb + h8 * 8, 8), waits=[w8])
                w_tY8[h8] = S.ms("vector")
                bgs(c, bkc, w_tY8[h8])
                w_lasttY = w_tY8[h8]
            w_gm4g = {}
            for hg in range(5):
                if hg < 4:
                    for hh in range(8):
                        h = hg * 8 + hh
                        bk = "EFCD"[hh % 4]
                        slo = 128 + (hh // 4) * 128
                        w = bgw(c, bk) if hh < 4 else []
                        S.op("tensor", "transpose", pb[bk][:, slo:slo + 128],
                             fbc(cumcT[:, hb + h:hb + h + 1]), c.idf, waits=w)
                    w7 = S.ms("tensor")
                    for hh in range(8):
                        h = hg * 8 + hh
                        bk = "EFCD"[hh % 4]
                        slo = 128 + (hh // 4) * 128
                        w2 = ([w7] + ([w_e4] if w_e4 is not None else [])) if hh == 0 else []
                        S.op("scalar", "activation", D4[:, hh * 128:(hh + 1) * 128],
                             pb[bk][:, slo:slo + 128], AF.Relu,
                             bias=cumcT[:, hb + h:hb + h + 1], scale=-1.0, waits=w2)
                        if hh >= 4:
                            bgs(c, bk, S.ms("scalar"))
                        else:
                            S.ms("scalar")
                    w_d4 = S.now("scalar")
                    w = [w_d4] + ([w_gm4] if w_gm4 is not None else [])
                    S.op("scalar", "activation", E4, D4, AF.Exp, bias=0.0,
                         scale=-1.0, waits=w)
                    w_e4 = S.ms("scalar")
                    gsl = Gm4[:, (hg % 2) * 1024:(hg % 2) * 1024 + 1024]
                    w = [w_e4] + ([w_gm4g[hg - 2]] if hg >= 2 else [])
                    S.op("vector", "tensor_mul", v3(gsl, 8), v3(E4, 8),
                         rep_mid(Gmsk, 8), waits=w)
                    w_gm4 = S.ms("vector")
                    w_gm4g[hg] = w_gm4
                if hg >= 1:
                    pg = hg - 1
                    gsl = Gm4[:, (pg % 2) * 1024:(pg % 2) * 1024 + 1024]
                    bky = "A" if pg % 2 == 0 else "B"
                    for hh in range(8):
                        h = pg * 8 + hh
                        w = bgw(c, bky) + [w_gm4g[pg]] if hh == 0 else []
                        if h == 0 and w_yprtp_last is not None:
                            w.append(w_yprtp_last)
                        S.op("tensor", "matmul", pb[bky][:, hh * 64:(hh + 1) * 64],
                             gsl[:, hh * 128:(hh + 1) * 128],
                             Uall[:, h * 64:(h + 1) * 64], start=True, stop=True,
                             waits=w)
                    w_ymm = S.ms("tensor")
                    S.op("vector", "tensor_add",
                         ypra[:, pg * 512:(pg + 1) * 512],
                         tYb[:, pg * 512:(pg + 1) * 512], pb[bky][:, 0:512],
                         waits=[w_ymm, w_tY8[pg]])
                    w_yc_last = S.ms("vector")
                    bgs(c, bky, w_yc_last)
            for h8 in range(4):
                bk = "A" if h8 % 2 == 0 else "B"
                for hh in range(8):
                    h = h8 * 8 + hh
                    w = bgw(c, bk) if hh == 0 else []
                    S.op("tensor", "matmul", pb[bk][:, hh * 64:(hh + 1) * 64], BT,
                         Xtall[:, h * 64:(h + 1) * 64], start=True, stop=True,
                         waits=w)
                w10 = S.ms("tensor")
                ssl = c.Sst[:, h8 * 512:(h8 + 1) * 512]
                S.op("vector", "tensor_mul", v3(tmpS, 8, 64), v3(ssl, 8, 64),
                     colrep(AtotB, hb + h8 * 8, 8))
                S.chain("vector")
                S.op("vector", "tensor_add", ssl, tmpS, pb[bk][:, 0:512],
                     waits=[w10])
                bgs(c, bk, S.ms("vector"))
            for bi, bkt in enumerate(("T1", "T2")):
                for sl8 in range(8):
                    c2 = bi * 8 + sl8
                    w = bgw(c, bkt) if sl8 == 0 else []
                    S.op("tensor", "transpose", pb[bkt][:, sl8 * 128:(sl8 + 1) * 128],
                         ypra[:, c2 * 128:(c2 + 1) * 128], c.idb, waits=w)
                w11 = S.ms("tensor")
                w_yprtp_last = w11
                c2b = bi * 8
                xp8 = bass.AP(xbc.tensor, xbc.offset + c2b * 515 + 3 + qq * 128,
                              [list(xbc.ap[0]), [515, 8], [1, 128]])
                y28 = bass.AP(y2.tensor, y2.offset + c2b * 512 + qq * 128,
                              [list(y2.ap[0]), [512, 8], [1, 128]])
                S.op("vector", "tensor_mul", v3(t8w, 8), xp8,
                     colrep(c.drep_sb, c2b, 8, 128))
                S.chain("vector")
                S.op("vector", "tensor_add", y28, v3(t8w, 8),
                     bass.AP(pb[bkt].tensor, pb[bkt].offset,
                             [list(pb[bkt].ap[0]), [128, 8], [1, 128]]),
                     waits=[w11])
                w_y2[(qq, c2b)] = S.ms("vector")
                for sl8 in range(8):
                    w_y2[(qq, c2b + sl8)] = w_y2[(qq, c2b)]
                bgs(c, bkt, w_y2[(qq, c2b)])
            w_uxread = S.now("tensor")

        if DBG_DUMP and mg == 0:
            dy2 = nc.dram_tensor("dbg_y2", [128, 16 * 512], BF16,
                                 kind="ExternalOutput").ap()
            dst8 = nc.dram_tensor("dbg_S", [128, NH * HP], F32,
                                  kind="ExternalOutput").ap()
            ddt = nc.dram_tensor("dbg_dtm", [32, 512], F32,
                                 kind="ExternalOutput").ap()
            S.dma("sync", dy2, y2, waits=[w_y2[(3, cc2)] for cc2 in range(16)], ch="dbg")
            S.dma("sync", dst8, c.Sst, waits=[S.now("vector")], ch="dbg")
            S.dma("sync", ddt, dtm, ch="dbg")
            S.barrier()
        w_g = {}
        for c2 in range(16):
            S.op("vector", "tensor_mul", sz[:, c2 * 512:(c2 + 1) * 512],
                 y2[:, c2 * 512:(c2 + 1) * 512], sz[:, c2 * 512:(c2 + 1) * 512],
                 waits=[w_y2[(3, c2)], w_ipz[15]] if c2 == 0 else [w_y2[(3, c2)]])
            w_g[c2] = S.ms("vector")
        w_g2 = {}
        for c2 in range(16):
            gsl2 = g2t[:, (c2 % 2) * 512:(c2 % 2) * 512 + 512]
            w = [] if c2 < 2 else [w_g2mm[c2 - 2]]
            S.op("vector", "tensor_mul", gsl2, sz[:, c2 * 512:(c2 + 1) * 512],
                 sz[:, c2 * 512:(c2 + 1) * 512], waits=w)
            w_g2[c2] = S.ms("vector")
            w = bgw(c, "D") if c2 == 0 else []
            S.op("tensor", "matmul", pb["D"][0:1, 0:512], c.onescb, gsl2,
                 start=(c2 == 0), stop=(c2 == 15), waits=[w_g2[c2]] + w)
            if c2 == 0:
                w_g2mm = {}
            w_g2mm[c2] = S.ms("tensor")
        w_ss = w_g2mm[15]
        S.op("scalar", "activation", rsr, pb["D"][0:1, 0:512], AF.Ln,
             bias=c.ebias[0:1, 0:1], scale=1.0 / D_INNER, waits=[w_ss])
        bgs(c, "D", S.chain("scalar"))
        S.op("scalar", "activation", rsr, rsr, AF.Exp, bias=0.0, scale=-0.5)
        w_rsr = S.ms("scalar")
        w = bgw(c, "E") + [w_rsr]
        S.op("tensor", "matmul", pb["E"][:, 0:512], c.onesrf, rsr,
             start=True, stop=True, waits=w)
        w_rsb = S.ms("tensor")
        w_gn = {}
        for c2 in range(16):
            S.op("vector", "scalar_tensor_tensor", y2[:, c2 * 512:(c2 + 1) * 512],
                 sz[:, c2 * 512:(c2 + 1) * 512], c.normw_sb[:, c2:c2 + 1],
                 pb["E"][:, 0:512], ALU.mult, ALU.mult, waits=[w_rsb])
            w_gn[c2] = S.ms("vector")
        bgs(c, "E", w_gn[15])
        w_res = {}
        for mo in range(NCH):
            bk = "ABEF"[mo % 4]
            w0 = bgw(c, bk) + [ld_xr]
            for kc in range(16):
                if mo == 0:
                    w = (w0 if kc == 0 else []) + [w_gn[kc]]
                else:
                    w = (w0 + [w_gn[15]]) if kc == 0 else []
                S.op("tensor", "matmul", pb[bk][:, 0:512],
                     c.woutb[:, kc * 1024 + mo * 128:kc * 1024 + (mo + 1) * 128],
                     y2[:, kc * 512:(kc + 1) * 512],
                     start=(kc == 0), stop=(kc == 15), waits=w)
            w_mm = S.ms("tensor")
            if mo == NCH - 1:
                w_zgate = w_mm
            S.op("vector", "tensor_add", xrt[:, mo * 512:(mo + 1) * 512],
                 xrt[:, mo * 512:(mo + 1) * 512], pb[bk][:, 0:512], waits=[w_mm])
            w_res[mo] = S.ms("vector")
            bgs(c, bk, w_res[mo])
        st_x = S.dma("sync", tile_view(c.xres, tlo, MEGA), xrt,
                     waits=[w_res[NCH - 1], w_res[NCH - 2]], ch="as")
        w_stmm = {}
        w_cast = {}
        for cc in range(NCH):
            xsl = y2[:, cc * 512:(cc + 1) * 512]
            x2sl = y2[:, (NCH + cc) * 512:(NCH + cc + 1) * 512]
            S.op("vector", "tensor_copy", xsl, xrt[:, cc * 512:(cc + 1) * 512],
                 waits=[w_res[cc]] + ([w_zgate] if cc == 0 else []))
            S.chain("vector")
            S.op("vector", "tensor_mul", x2sl, xsl, xsl)
            w_cast[cc] = S.ms("vector")
        for cc in range(NCH):
            xsl = y2[:, cc * 512:(cc + 1) * 512]
            x2sl = y2[:, (NCH + cc) * 512:(NCH + cc + 1) * 512]
            w = ([w_cast[NCH - 1]] + bgw(c, "C") + [w_lasttY]) if cc == 0 else []
            S.op("tensor", "matmul", pb["C"][0:1, 0:512], c.onescb, xsl,
                 start=(cc == 0), stop=(cc == 7), waits=w)
            w2 = bgw(c, "D") if cc == 0 else []
            S.op("tensor", "matmul", pb["D"][0:1, 0:512], c.onescb, x2sl,
                 start=(cc == 0), stop=(cc == 7), waits=w2)
            w_stmm[cc] = S.ms("tensor")
        w_st = w_stmm[NCH - 1]
        w = [w_st, st_x] + ([w_r12] if mg > 0 else [])
        S.op("vector", "tensor_copy", c.r1, pb["C"][0:1, 0:512], waits=w)
        bgs(c, "C", S.ms("vector"))
        S.op("vector", "tensor_copy", c.r2, pb["D"][0:1, 0:512])
        w_xrt_done = S.ms("vector")
        bgs(c, "D", w_xrt_done)
        w_r12 = S.dma("sync", c.stats_h[0][None, tlo:tlo + MEGA], c.r1,
                      waits=[w_xrt_done], ch="ast")
        w_r12 = S.dma("sync", c.stats_h[1][None, tlo:tlo + MEGA], c.r2, ch="ast")

    stats_to_mrs(c, w_r12)
    stA.close()
    c.stW.close()


def stats_to_mrs(c, w_last, dim=DM):
    S = c.S
    w_pr = None
    for g in range(NMEGA):
        lo = g * MEGA
        ld1 = S.dma("sync", c.r1, c.stats_h[0][None, lo:lo + MEGA],
                    waits=[w_last] if g == 0 else [w_pr], ch="ml1")
        ld2 = S.dma("sync", c.r2, c.stats_h[1][None, lo:lo + MEGA], ch="ml2")
        S.op("scalar", "activation", c.r1, c.r1, AF.Copy, scale=1.0 / dim,
             waits=[ld1])
        S.chain("scalar")
        S.op("scalar", "activation", c.r3, c.r1, AF.Square)
        w_s = S.ms("scalar")
        S.op("vector", "scalar_tensor_tensor", c.r3, c.r2, 1.0 / dim, c.r3,
             ALU.mult, ALU.subtract, waits=[w_s, ld2])
        w_v = S.ms("vector")
        S.op("scalar", "activation", c.r3, c.r3, AF.Ln, bias=c.ebias[0:1, 0:1],
             scale=1.0, waits=[w_v])
        S.chain("scalar")
        S.op("scalar", "activation", c.r3, c.r3, AF.Exp, bias=0.0, scale=-0.5)
        w_rs = S.ms("scalar")
        S.dma("sync", c.stats_h[2][None, lo:lo + MEGA], c.r1, waits=[w_rs], ch="mrs")
        w_pr = S.dma("sync", c.stats_h[3][None, lo:lo + MEGA], c.r3, ch="mrs")
    c.w_lnrows = w_pr


def phaseB(c, tag):
    S, nc = c.S, c.nc
    pb = c.pbank
    stB = ExitStack()

    def al(nm, sh, dt):
        return stB.enter_context(nc.sbuf_tensor(nm + str(tag), sh, dt)).ap()

    wr_sb = al("wr_sb", [128, NCH * 1024], BF16)
    wk_sb = al("wk_sb", [128, NCH * 1024], BF16)
    wv_sb = al("wv_sb", [128, NCH * 1024], BF16)
    xrt = al("xrtB", [128, NCH * 512], F32)
    xn2 = al("xn2", [128, NCH * 513], BF16)
    mb_t = al("mb_t", [128, 512], BF16)
    rs_t = al("rs_t", [128, 512], BF16)
    t1b = al("t1b", [128, 2 * 512], BF16)
    dxb = al("dxb", [128, 2 * 512], BF16)
    r_t = al("r_t", [128, NCH * 512], BF16)
    ka_t = al("ka_t", [128, NCH * 512], BF16)
    kact = al("kact", [128, NCH * 512], BF16)
    kk_t = al("kk_t", [128, 2 * 512], BF16)
    o_t = al("o_t", [128, 2 * 512], F32)
    sig_t = al("sig_t", [128, NCH * 512], BF16)
    h1 = al("h1", [128, NCH], BF16)
    xb_t = al("xb_tB", [128, 2 * 512], BF16)
    x2b = al("x2bB", [128, 2 * 512], BF16)
    mrow = al("mrow", [1, 512], F32)
    rsrow = al("rsrow", [1, 512], F32)
    otm = al("otm", [128, 2 * DM], F32) if tag == 2 else None
    w_ost = {}

    wstg = al("wstg", [128, 4 * 1024], F32)
    w_wld = {}
    for k in range(24):
        wi, kc = k // NCH, k % NCH
        wsb = (wr_sb, wk_sb, wv_sb)[wi]
        wsrc = (c.wrT, c.wkT, c.wvT)[wi]
        b = k % 4
        sl = wstg[:, b * 1024:(b + 1) * 1024]
        w = [w_wld[((k - 4) // NCH, (k - 4) % NCH)]] if k >= 4 else []
        ld = S.dma("sync", sl, wsrc[kc * 128:(kc + 1) * 128, :], waits=w,
                   ch="bw%d" % b)
        S.op("scalar", "copy", wsb[:, kc * 1024:(kc + 1) * 1024], sl, waits=[ld])
        w_wld[(wi, kc)] = S.ms("scalar")
    w_wl = w_wld[(2, NCH - 1)]
    if tag == 1:
        c.wqkvb_h = nc.dram_tensor("wqkvb_h", [24, 128, NCH * 128], BF16,
                                   kind="Internal").ap()
        c.wpb_h = nc.dram_tensor("wpb_h", [NCH, 128, NCH * 128], BF16,
                                 kind="Internal").ap()
        wtmpC = al("wtmpC", [128, 2 * 1024], BF16)
        stq = {}
        for m in range(24):
            b = m % 2
            sl = wtmpC[:, b * 1024:(b + 1) * 1024]
            w = [stq[m - 2]] if m >= 2 else []
            ldw = S.dma("gpsimd", sl, kchunk_view(c.wqkv_l[m]), waits=w,
                        ch="qr%d" % b)
            stq[m] = S.dma("gpsimd", c.wqkvb_h[m], sl, waits=[ldw], ch="qs%d" % b)
        for m in range(NCH):
            b = m % 2
            sl = wtmpC[:, b * 1024:(b + 1) * 1024]
            src = bass.AP(c.wproj.tensor, c.wproj.offset + m * 128,
                          [[DM, 128], [128 * DM, NCH], [1, 128]])
            ldw = S.dma("gpsimd", sl, src, waits=[stq[22 + m]], ch="qr%d" % b)
            stq[24 + m] = S.dma("gpsimd", c.wpb_h[m], sl, waits=[ldw],
                                ch="qs%d" % b)
    S.op("vector", "memset", h1, 0.0)
    w_h1 = S.ms("vector")

    w_xrt_done = None
    w_vlast = None
    w_bc_prev = None
    st_x = None
    w_stmm = {}
    for mg in range(NMEGA):
        tlo = mg * MEGA
        w = ([w_xrt_done] if w_xrt_done else []) + ([st_x] if st_x else [])
        ld_xr = S.dma("sync", xrt, tile_view(c.xres, tlo, MEGA), waits=w, ch="br")
        w = [c.w_lnrows] if mg == 0 else [w_bc_prev]
        ld_m = S.dma("sync", mrow, c.stats_h[2][None, tlo:tlo + MEGA], waits=w, ch="bm1")
        ld_rs = S.dma("sync", rsrow, c.stats_h[3][None, tlo:tlo + MEGA], ch="bm2")
        w = bgw(c, "C") + [ld_m] + ([w_h1] if mg == 0 else [])
        S.op("tensor", "matmul", pb["C"][:, 0:512], c.onesrf, mrow,
             start=True, stop=True, waits=w)
        w = bgw(c, "D") + [ld_rs]
        S.op("tensor", "matmul", pb["D"][:, 0:512], c.onesrf, rsrow,
             start=True, stop=True, waits=w)
        w_bc = S.ms("tensor")
        w_bc_prev = w_bc
        S.op("vector", "tensor_copy", mb_t, pb["C"][:, 0:512], waits=[w_bc])
        bgs(c, "C", S.now("vector"))
        S.op("vector", "tensor_copy", rs_t, pb["D"][:, 0:512])
        bgs(c, "D", S.chain("vector"))
        w_mix = {}
        for cc in range(NCH):
            xsl = xn2[:, cc * 513:cc * 513 + 513]
            tsl = t1b[:, (cc % 2) * 512:(cc % 2) * 512 + 512]
            dsl = dxb[:, (cc % 2) * 512:(cc % 2) * 512 + 512]
            w = [ld_xr]
            if w_vlast is not None and cc == 0:
                w.append(w_vlast)
            S.op("vector", "tensor_copy", xsl[:, 0:1], h1[:, cc:cc + 1], waits=w)
            S.chain("vector")
            S.op("vector", "tensor_sub", tsl, xrt[:, cc * 512:(cc + 1) * 512], mb_t)
            S.chain("vector")
            S.op("vector", "tensor_mul", xsl[:, 1:513], tsl, rs_t)
            S.chain("vector")
            S.op("vector", "tensor_sub", dsl, xsl[:, 1:513], xsl[:, 0:512])
            S.chain("vector")
            S.op("vector", "scalar_tensor_tensor", tsl, dsl, c.mur_sb[:, cc:cc + 1],
                 xsl[:, 0:512], ALU.mult, ALU.add)
            S.chain("vector")
            S.op("vector", "tensor_copy", r_t[:, cc * 512:(cc + 1) * 512], tsl)
            S.chain("vector")
            S.op("vector", "scalar_tensor_tensor", ka_t[:, cc * 512:(cc + 1) * 512],
                 dsl, c.muk_sb[:, cc:cc + 1], xsl[:, 0:512], ALU.mult, ALU.add)
            w_mix[cc] = S.ms("vector")
            S.op("vector", "tensor_copy", h1[:, cc:cc + 1], xsl[:, 512:513])
            S.chain("vector")
        w_mixall = S.now("vector")
        w_rmm = {}
        w_ka = {}
        for mo in range(NCH):
            bk = "AE"[mo % 2]
            w = bgw(c, bk) + ([w_mixall] if mo == 0 else [])
            for kc in range(NCH):
                wk2 = (w if kc == 0 else []) + \
                    ([w_wld[(0, kc)]] if mg == 0 else [])
                S.op("tensor", "matmul", pb[bk][:, 0:512],
                     wr_sb[:, kc * 1024 + mo * 128:kc * 1024 + (mo + 1) * 128],
                     r_t[:, kc * 512:(kc + 1) * 512],
                     start=(kc == 0), stop=(kc == 7), waits=wk2)
            w_rmm[mo] = S.ms("tensor")
            S.op("scalar", "activation", sig_t[:, mo * 512:(mo + 1) * 512],
                 pb[bk][:, 0:512], AF.Sigmoid, waits=[w_rmm[mo]])
            bgs(c, bk, S.ms("scalar"))
            bk = "BF"[mo % 2]
            w = bgw(c, bk) + ([w_mixall] if mo == 0 else [])
            for kc in range(NCH):
                wk2 = (w if kc == 0 else []) + \
                    ([w_wld[(1, kc)]] if mg == 0 else [])
                S.op("tensor", "matmul", pb[bk][:, 0:512],
                     wk_sb[:, kc * 1024 + mo * 128:kc * 1024 + (mo + 1) * 128],
                     ka_t[:, kc * 512:(kc + 1) * 512],
                     start=(kc == 0), stop=(kc == 7), waits=wk2)
            w_kmm = S.ms("tensor")
            ksl = kk_t[:, (mo % 2) * 512:(mo % 2) * 512 + 512]
            w = [w_kmm] + ([w_ka[mo - 2]] if mo >= 2 else [])
            S.op("vector", "tensor_scalar", ksl, pb[bk][:, 0:512], 0.0, None,
                 ALU.max, waits=w)
            bgs(c, bk, S.chain("vector"))
            w = [w_vlast] if (mo == 0 and w_vlast is not None) else []
            S.op("vector", "tensor_mul", kact[:, mo * 512:(mo + 1) * 512], ksl, ksl,
                 waits=w)
            w_ka[mo] = S.ms("vector")
        w_sig = S.now("scalar")
        w_res = {}
        for mo in range(NCH):
            bk = "CD"[mo % 2]
            w0 = bgw(c, bk)
            for kc in range(NCH):
                if mo == 0:
                    w = (w0 if kc == 0 else []) + [w_ka[kc]]
                else:
                    w = (w0 + [w_ka[NCH - 1]]) if kc == 0 else []
                if mg == 0:
                    w = w + [w_wld[(2, kc)]]
                S.op("tensor", "matmul", pb[bk][:, 0:512],
                     wv_sb[:, kc * 1024 + mo * 128:kc * 1024 + (mo + 1) * 128],
                     kact[:, kc * 512:(kc + 1) * 512],
                     start=(kc == 0), stop=(kc == 7), waits=w)
            w_vmm = S.ms("tensor")
            w_vlast = w_vmm
            osl = o_t[:, (mo % 2) * 512:(mo % 2) * 512 + 512]
            w = [w_vmm, w_sig] + ([w_res[mo - 2]] if mo >= 2 else [])
            S.op("vector", "tensor_mul", osl, sig_t[:, mo * 512:(mo + 1) * 512],
                 pb[bk][:, 0:512], waits=w)
            bgs(c, bk, S.chain("vector"))
            S.op("vector", "tensor_add", xrt[:, mo * 512:(mo + 1) * 512],
                 xrt[:, mo * 512:(mo + 1) * 512], osl)
            w_res[mo] = S.ms("vector")
        if DBG_DUMP and tag == 2 and mg == 0:
            dxn2 = nc.dram_tensor("dbg_xn2", [128, NCH * 513], BF16,
                                  kind="ExternalOutput").ap()
            dsig = nc.dram_tensor("dbg_sig", [128, NCH * 512], BF16,
                                  kind="ExternalOutput").ap()
            dka = nc.dram_tensor("dbg_ka", [128, NCH * 512], BF16,
                                 kind="ExternalOutput").ap()
            dxr2 = nc.dram_tensor("dbg_xr2", [128, NCH * 512], F32,
                                  kind="ExternalOutput").ap()
            S.dma("sync", dxn2, xn2, waits=[w_res[NCH - 1]], ch="dbg")
            S.dma("sync", dsig, sig_t, ch="dbg")
            S.dma("sync", dka, r_t, ch="dbg")
            S.dma("sync", dxr2, xrt, ch="dbg")
            S.barrier()
        if tag == 2:
            for tt in range(4):
                for cc2 in range(NCH):
                    bkf = "EFCD"[cc2 % 4]
                    w = bgw(c, bkf) + ([w_res[cc2]] if tt == 0 else [])
                    S.op("tensor", "transpose", pb[bkf][:, 0:128],
                         xrt[:, cc2 * 512 + tt * 128:cc2 * 512 + (tt + 1) * 128],
                         c.idf, waits=w)
                    w_tp = S.ms("tensor")
                    w2 = [w_tp] + ([w_ost[tt % 2]] if (cc2 == 0 and (tt % 2) in w_ost)
                                   else [])
                    S.op("scalar", "copy",
                         otm[:, (tt % 2) * DM + cc2 * 128:(tt % 2) * DM + (cc2 + 1) * 128],
                         pb[bkf][:, 0:128], waits=w2)
                    bgs(c, bkf, S.ms("scalar"))
                w_sc = S.now("scalar")
                w_ost[tt % 2] = S.dma(
                    "sync", c.y_out[tlo + tt * 128:tlo + (tt + 1) * 128, :],
                    otm[:, (tt % 2) * DM:(tt % 2 + 1) * DM], waits=[w_sc],
                    ch="es%d" % (tt % 2))
            st_x = w_ost[(4 - 1) % 2]
        else:
            st_x = S.dma("sync", tile_view(c.xres, tlo, MEGA), xrt,
                         waits=[w_res[NCH - 1], w_res[NCH - 2]], ch="bst")
        if tag == 1:
            for cc in range(NCH):
                xsl2 = xb_t[:, (cc % 2) * 512:(cc % 2) * 512 + 512]
                x2sl = x2b[:, (cc % 2) * 512:(cc % 2) * 512 + 512]
                w = [w_res[cc]] + ([w_stmm[cc - 2]] if cc >= 2 else [])
                S.op("vector", "tensor_copy", xsl2, xrt[:, cc * 512:(cc + 1) * 512],
                     waits=w)
                S.chain("vector")
                S.op("vector", "tensor_mul", x2sl, xsl2, xsl2)
                wp = S.ms("vector")
                w = bgw(c, "C") if cc == 0 else []
                S.op("tensor", "matmul", pb["C"][0:1, 0:512], c.onescb, xsl2,
                     start=(cc == 0), stop=(cc == 7), waits=[wp] + w)
                w2 = bgw(c, "D") if cc == 0 else []
                S.op("tensor", "matmul", pb["D"][0:1, 0:512], c.onescb, x2sl,
                     start=(cc == 0), stop=(cc == 7), waits=w2)
                w_stmm[cc] = S.ms("tensor")
            w_st = w_stmm[NCH - 1]
            w = [w_st, st_x] + ([c.w_r12B] if mg > 0 else [])
            S.op("vector", "tensor_copy", c.r1, pb["C"][0:1, 0:512], waits=w)
            bgs(c, "C", S.ms("vector"))
            S.op("vector", "tensor_copy", c.r2, pb["D"][0:1, 0:512])
            w_xrt_done = S.ms("vector")
            bgs(c, "D", w_xrt_done)
            c.w_r12B = S.dma("sync", c.stats_h[0][None, tlo:tlo + MEGA], c.r1,
                             waits=[w_xrt_done], ch="bss")
            c.w_r12B = S.dma("sync", c.stats_h[1][None, tlo:tlo + MEGA], c.r2, ch="bss")
        else:
            w_xrt_done = w_res[NCH - 1]
    stB.close()


def phaseC(c):
    S, nc = c.S, c.nc
    pb = c.pbank
    stC = ExitStack()

    def al(nm, sh, dt):
        return stC.enter_context(nc.sbuf_tensor(nm, sh, dt)).ap()

    stats_to_mrs(c, None)

    K_sb = al("K_sb", [128, NCH * T], BF16)
    vb = al("vb", [128, 32 * 1040], BF16)
    permb = al("permb", [128, 128], BF16)
    battnb = al("battnb", [1, 24 * 128], BF16)
    bprojb = al("bprojb", [1, NCH * 128], BF16)
    ones512 = al("ones512", [1, 512], BF16)
    wq_sb = al("wq_sb", [128, 3 * NCH * 128], BF16)
    xrt = al("xrtC", [128, NCH * 512], F32)
    st1 = ExitStack()

    def al1(nm, sh, dt):
        return st1.enter_context(nc.sbuf_tensor(nm, sh, dt)).ap()

    cos_sb = al1("cos_sb", [128, 2 * 512], BF16)
    sin_sb = al1("sin_sb", [128, 2 * 512], BF16)
    xn3 = al1("xn3", [128, NCH * 512], BF16)
    q_st = al1("q_st", [128, NCH * 512], BF16)
    mb_t = al1("mb_tC", [128, 512], BF16)
    rs_t = al1("rs_tC", [128, 512], BF16)
    tr1 = al1("tr1", [128, 2 * 512], BF16)
    tr2 = al1("tr2", [128, 2 * 512], BF16)
    tr3 = al1("tr3", [128, 2 * 512], BF16)
    mrow = al1("mrowC", [1, 512], F32)
    rsrow = al1("rsrowC", [1, 512], F32)

    w_pc = S.dma("gpsimd", permb, c.c_perm, ch="cg")
    S.dma("gpsimd", battnb, c.battn_r, ch="cg")
    w_bb = S.dma("gpsimd", bprojb, c.bproj_r, ch="cg")
    S.op("vector", "memset", ones512, 1.0)
    w_o512 = S.ms("vector")
    wqkvb_h = c.wqkvb_h
    wpb_h = c.wpb_h

    S.op("vector", "memset",
         bass.AP(vb.tensor, vb.offset + 64, [list(vb.ap[0]), [65, 16 * 32], [1, 1]]),
         1.0)
    w_vones = S.ms("vector")

    w_xrt_done = None
    st_q = None
    w_win_used = {}
    w_kvdone = []
    w_bc_prev = None
    w_rope_slot = {}
    for mg in range(NMEGA):
        tlo = mg * MEGA
        w = ([w_xrt_done] if w_xrt_done else [])
        ld_xr = S.dma("sync", xrt, tile_view(c.xres, tlo, MEGA), waits=w, ch="cr")
        w = [c.w_lnrows] if mg == 0 else [w_bc_prev]
        ld_m = S.dma("sync", mrow, c.stats_h[2][None, tlo:tlo + MEGA], waits=w, ch="cm1")
        ld_rs = S.dma("sync", rsrow, c.stats_h[3][None, tlo:tlo + MEGA], ch="cm2")
        S.dma("gpsimd", cos_sb[:, (mg % 2) * 512:(mg % 2) * 512 + 512],
              c.c_cos[:, tlo:tlo + MEGA], waits=[S.now("vector")] if mg >= 2 else [],
              ch="ct%d" % (mg % 2))
        ld_cs = S.dma("gpsimd", sin_sb[:, (mg % 2) * 512:(mg % 2) * 512 + 512],
              c.c_sin[:, tlo:tlo + MEGA], ch="ct%d" % (mg % 2))
        cssl = cos_sb[:, (mg % 2) * 512:(mg % 2) * 512 + 512]
        snsl = sin_sb[:, (mg % 2) * 512:(mg % 2) * 512 + 512]
        w = bgw(c, "C") + [ld_m]
        S.op("tensor", "matmul", pb["C"][:, 0:512], c.onesrf, mrow,
             start=True, stop=True, waits=w)
        w = bgw(c, "D") + [ld_rs]
        S.op("tensor", "matmul", pb["D"][:, 0:512], c.onesrf, rsrow,
             start=True, stop=True, waits=w)
        w_bc = S.ms("tensor")
        w_bc_prev = w_bc
        S.op("vector", "tensor_copy", mb_t, pb["C"][:, 0:512], waits=[w_bc])
        bgs(c, "C", S.now("vector"))
        S.op("vector", "tensor_copy", rs_t, pb["D"][:, 0:512])
        bgs(c, "D", S.chain("vector"))
        w_xn = {}
        for cc in range(NCH):
            tsl = tr1[:, (cc % 2) * 512:(cc % 2) * 512 + 512]
            w = [ld_xr]
            if mg > 0 and cc == 0:
                w.append(w_qkv_used)
            S.op("vector", "tensor_sub", tsl, xrt[:, cc * 512:(cc + 1) * 512], mb_t,
                 waits=w)
            S.chain("vector")
            S.op("vector", "tensor_mul", xn3[:, cc * 512:(cc + 1) * 512], tsl, rs_t)
            w_xn[cc] = S.chain("vector")
        w_xnall = S.now("vector")
        for m in range(24):
            b = m % 3
            wsl = wq_sb[:, b * 1024:(b + 1) * 1024]
            gm = mg * 24 + m
            w = [w_win_used[gm - 3]] if gm >= 3 else []
            ld_w = S.dma("sync", wsl, wqkvb_h[m], waits=w, ch="cw%d" % b)
            bk = "AB"[m % 2]
            w = bgw(c, bk) + [ld_w, w_xnall]
            if gm == 0:
                w += [w_bb, w_o512]
            for kc in range(NCH):
                S.op("tensor", "matmul", pb[bk][:, 0:512],
                     wsl[:, kc * 128:(kc + 1) * 128], xn3[:, kc * 512:(kc + 1) * 512],
                     start=(kc == 0), stop=False, waits=w if kc == 0 else [])
            S.op("tensor", "matmul", pb[bk][:, 0:512],
                 battnb[0:1, m * 128:(m + 1) * 128], ones512,
                 start=False, stop=True)
            w_mm = S.ms("tensor")
            w_win_used[gm] = w_mm
            cq = m % 8
            if m < 16:
                qtmp = tr2[:, (m % 2) * 512:(m % 2) * 512 + 512]
                w = [w_mm] + ([w_rope_slot[m % 2]] if (m % 2) in w_rope_slot else [])
                S.op("scalar", "activation", qtmp, pb[bk][:, 0:512], AF.Copy,
                     waits=w)
                w_ev = S.ms("scalar")
                bgs(c, bk, w_ev)
                bkp = "E" if m % 2 == 0 else "F"
                w = bgw(c, bkp) + [w_ev]
                S.op("tensor", "matmul", pb[bkp][:, 0:512], permb, qtmp,
                     start=True, stop=True, waits=w)
                w_pm = S.ms("tensor")
                tsl = tr1[:, (m % 2) * 512:(m % 2) * 512 + 512]
                t2sl = tr3[:, (m % 2) * 512:(m % 2) * 512 + 512]
                w = [ld_cs, w_ev]
                if m == 0 and st_q is not None:
                    w.append(st_q)
                S.op("vector", "tensor_mul", tsl, qtmp, cssl, waits=w)
                S.chain("vector")
                S.op("vector", "tensor_mul", t2sl, pb[bkp][:, 0:512], snsl,
                     waits=[w_pm])
                S.chain("vector")
                dst = (q_st[:, cq * 512:(cq + 1) * 512] if m < 8 else
                       K_sb[:, cq * T + tlo:cq * T + tlo + MEGA])
                S.op("vector", "tensor_add", dst, tsl, t2sl)
                w_rope = S.ms("vector")
                w_rope_slot[m % 2] = w_rope
                bgs(c, bkp, w_rope)
            else:
                vtmp = tr2[:, (m % 2) * 512:(m % 2) * 512 + 512]
                w = [w_mm] + ([w_rope_slot[m % 2]] if (m % 2) in w_rope_slot else [])
                S.op("scalar", "activation", vtmp, pb[bk][:, 0:512], AF.Copy,
                     waits=w)
                w_ev = S.ms("scalar")
                bgs(c, bk, w_ev)
                for tt in range(4):
                    bkt = "T1" if tt % 2 == 0 else "T2"
                    kt = tlo // 128 + tt
                    w = bgw(c, bkt) + ([w_ev] if tt < 2 else [])
                    S.op("tensor", "transpose", pb[bkt][:, 0:128],
                         vtmp[:, tt * 128:(tt + 1) * 128], c.idb, waits=w)
                    w_tp = S.ms("tensor")
                    dst = bass.AP(vb.tensor, vb.offset + kt * 1040 + (2 * cq) * 65,
                                  [list(vb.ap[0]), [65, 2], [1, 64]])
                    src = bass.AP(pb[bkt].tensor, pb[bkt].offset,
                                  [list(pb[bkt].ap[0]), [64, 2], [1, 64]])
                    S.op("vector", "tensor_copy", dst, src,
                         waits=[w_tp] + ([w_vones] if mg == 0 and m == 16 and tt == 0 else []))
                    bgs(c, bkt, S.ms("vector"))
        w_qkv_used = S.now("tensor")
        w_kvdone = [S.now("vector"), S.now("scalar")]
        w = [S.now("vector")] + ([st_q] if st_q else [])
        st_q = S.dma("sync", tile_view(c.q_hbm, tlo, MEGA), q_st, waits=w, ch="cq")
        w_xrt_done = w_qkv_used

    w_c1done = [S.now("tensor"), S.now("vector"), S.now("scalar")]
    if DBG_DUMP:
        dK = nc.dram_tensor("dbg_K", [128, NCH * T], BF16, kind="ExternalOutput").ap()
        dvb = nc.dram_tensor("dbg_vb", [128, 32 * 1040], BF16, kind="ExternalOutput").ap()
        dq = nc.dram_tensor("dbg_q", [NCH, 128, T], BF16, kind="ExternalOutput").ap()
        S.dma("sync", dK, K_sb, waits=w_c1done + [st_q], ch="dbg")
        S.dma("sync", dvb, vb, ch="dbg")
        S.dma("sync", dq, c.q_hbm, ch="dbg")
        S.barrier()
    st1.close()
    qq_sb = al("qq_sb", [128, 2 * NCH * 512], BF16)
    pt_sb = al("pt_sb", [128, 6 * 512], BF16)
    yraw = al("yraw", [128, NCH * 512], BF16)
    den_t = al("den_t", [1, 2 * 512], F32)
    den16 = al("den16", [128, 64], F32)
    denb_t = al("denb_t", [1, 2 * 512], BF16)
    xb_t = yraw[:, 0:2 * 512]
    x2b = yraw[:, 2 * 512:4 * 512]
    denb_h = c.denb_h

    w_den_st = {}
    w_stmm = {}
    w_oslot = {}
    st_db = None
    st_x = None
    w_rb = None
    ld_d16_prev = None
    w_xrt2 = None
    w_pv_slot = {}
    w_wp_used = {}
    w_q_used = {}
    for qm in range(NMEGA):
        qlo = qm * MEGA
        qsl = qq_sb[:, (qm % 2) * NCH * 512:(qm % 2 + 1) * NCH * 512]
        w = [st_q] + (w_c1done if qm == 0 else []) + \
            ([w_q_used[qm - 2]] if qm >= 2 else [])
        ld_qq = S.dma("sync", qsl, tile_view(c.q_hbm, qlo, MEGA), waits=w, ch="c2q")
        w = ([w_xrt2] if w_xrt2 else [w_xrt_done]) + ([st_x] if st_x else [])
        ld_xr = S.dma("sync", xrt, tile_view(c.xres, qlo, MEGA), waits=w, ch="cr")
        nkt = 4 * qm + 4
        for cc in range(NCH):
            h0 = 2 * cc
            w_exp = {}
            pvargs = {}
            for kt in range(nkt + 1):
                if kt < nkt:
                    off = max(0, (kt - 4 * qm) * 128)
                    N = 512 - off
                    s0 = 2 * (kt % 3)
                    pair = "EF" if kt % 2 == 0 else "CD"
                    for half in range(2):
                        bks = ("E", "F", "C", "D")[2 * (kt % 2) + half]
                        rows = slice(half * 64, half * 64 + 64)
                        w = bgw(c, bks)
                        if cc == 0 and kt == 0 and qm == 0:
                            w += w_kvdone + [ld_qq]
                        if kt <= 1:
                            w += [ld_qq]
                        S.op("tensor", "matmul", pb[bks][:, 0:N],
                             K_sb[rows, cc * T + kt * 128:cc * T + (kt + 1) * 128],
                             qsl[rows, cc * 512 + off:(cc + 1) * 512],
                             start=True, stop=True,
                             tile_position=(half * 64, 0), waits=w)
                    w_s = S.ms("tensor")
                    wl = [w_s]
                    for half in range(2):
                        if (kt % 3, half) in w_pv_slot:
                            wl.append(w_pv_slot[(kt % 3, half)])
                    if off <= 128:
                        S.op("scalar", "activation",
                             pt_sb[:, s0 * 512:s0 * 512 + 512 + N],
                             pb[pair][:, 0:512 + N], AF.Exp,
                             bias=c.ebias[:, 1:2], scale=0.125, waits=wl)
                        w_es = [S.ms("scalar")] * 2
                    else:
                        w_es = []
                        for half in range(2):
                            S.op("scalar", "activation",
                                 pt_sb[:, (s0 + half) * 512:(s0 + half) * 512 + N],
                                 pb[pair][:, half * 512:half * 512 + N], AF.Exp,
                                 bias=c.ebias[:, 1:2], scale=0.125,
                                 waits=wl if half == 0 else [])
                            w_es.append(S.ms("scalar"))
                    bgs(c, pair[0], w_es[1])
                    bgs(c, pair[1], w_es[1])
                    for half in range(2):
                        psl = pt_sb[:, (s0 + half) * 512:(s0 + half) * 512 + N]
                        w_e = w_es[half]
                        if kt >= 4 * qm:
                            S.op("vector", "tensor_mul", psl[:, 0:128], psl[:, 0:128],
                                 c.mask01b, waits=[w_e])
                            w_e = S.ms("vector")
                        w_exp[(kt, half)] = w_e
                        pvargs[(kt, half)] = (off, psl)
                if kt >= 1:
                    pkt = kt - 1
                    for half in range(2):
                        h = h0 + half
                        bky = "A" if half == 0 else "B"
                        off, psl = pvargs.pop((pkt, half))
                        wpv = [w_exp.pop((pkt, half))]
                        if pkt == 0:
                            wpv += bgw(c, bky)
                        S.op("tensor", "matmul", pb[bky][0:65, off:512],
                             vb[:, pkt * 1040 + h * 65:pkt * 1040 + h * 65 + 65],
                             psl, start=(pkt == 0), stop=(pkt == nkt - 1),
                             skip_group_check=True, waits=wpv)
                        w_pv_slot[(pkt % 3, half)] = S.ms("tensor")
            w_pvA = w_pv_slot.get(((nkt - 1) % 3, 0))
            w_pvB = w_pv_slot.get(((nkt - 1) % 3, 1))
            w_lastS = w_pvB
            w = [w_pvA, w_pvB]
            if cc >= 1:
                w.append(w_den_st.get(cc - 1))
            if cc == 0 and ld_d16_prev is not None:
                w.append(ld_d16_prev)
            S.op("vector", "tensor_copy", yraw[0:64, cc * 512:(cc + 1) * 512],
                 pb["A"][0:64, 0:512], waits=w)
            S.op("vector", "tensor_copy", yraw[64:128, cc * 512:(cc + 1) * 512],
                 pb["B"][0:64, 0:512])
            S.op("vector", "tensor_copy", den_t[0:1, 0:512], pb["A"][64:65, 0:512])
            S.op("vector", "tensor_copy", den_t[0:1, 512:1024], pb["B"][64:65, 0:512])
            w_dv = S.ms("vector")
            bgs(c, "A", w_dv)
            bgs(c, "B", w_dv)
            S.dma("sync", c.den_h[2 * cc], den_t[0:1, 0:512], waits=[w_dv], ch="cd")
            w_den_st[cc] = S.dma("sync", c.den_h[2 * cc + 1], den_t[0:1, 512:1024], ch="cd")
        if DBG_DUMP and qm == 1:
            dyr = nc.dram_tensor("dbg_yraw", [128, NCH * 512], BF16,
                                 kind="ExternalOutput").ap()
            dden = nc.dram_tensor("dbg_den", [16, 512], F32,
                                  kind="ExternalOutput").ap()
            S.dma("sync", dyr, yraw, waits=[S.now("vector")], ch="dbg")
            S.dma("sync", dden, c.den_h, ch="dbg")
            S.barrier()
        w = [w_den_st[NCH - 1]] + ([st_db] if qm > 0 else [])
        ld_d16 = S.dma("sync", den16, c.den_h, waits=w, ch="cdl")
        ld_d16_prev = ld_d16
        S.op("vector", "reciprocal", den16, den16, waits=[ld_d16])
        w_rc = S.ms("vector")
        st_db = S.dma("gpsimd", denb_h, den16, waits=[w_rc], ch="cdb")
        w_res = {}
        w_norm = {}
        w_ores = {}
        for cc in range(NCH):
            w = [st_db] + ([w_rb] if (qm + cc) > 0 else [])
            ldb = S.dma("sync", denb_t[0:1, 0:512],
                        c.denb_h[0:1, (2 * cc) * 512:(2 * cc + 1) * 512], waits=w, ch="cdn")
            w_ldb = S.dma("sync", denb_t[0:1, 512:1024],
                          c.denb_h[0:1, (2 * cc + 1) * 512:(2 * cc + 2) * 512], ch="cdn")
            rbk = "AB"[cc % 2]
            w = bgw(c, rbk) + [ldb, w_ldb]
            S.op("tensor", "matmul", pb[rbk][0:64, 0:512], c.onesrb[0:1, 0:64],
                 denb_t[0:1, 0:512], start=True, stop=True, waits=w)
            S.op("tensor", "matmul", pb[rbk][64:128, 0:512], c.onesrb[0:1, 0:64],
                 denb_t[0:1, 512:1024], start=True, stop=True,
                 tile_position=(0, 64))
            w_rb = S.ms("tensor")
            S.op("vector", "tensor_mul", yraw[:, cc * 512:(cc + 1) * 512],
                 yraw[:, cc * 512:(cc + 1) * 512], pb[rbk][:, 0:512],
                 waits=[w_rb])
            w_norm[cc] = S.ms("vector")
            bgs(c, rbk, w_norm[cc])
        for mo in range(NCH):
            b = mo % 3
            wsl = wq_sb[:, b * 1024:(b + 1) * 1024]
            gm2 = qm * NCH + mo
            w = [w_wp_used[gm2 - 3]] if gm2 >= 3 else []
            ld_w = S.dma("sync", wsl, wpb_h[mo], waits=w, ch="cw%d" % b)
            bk = "AB"[mo % 2]
            w0 = bgw(c, bk) + [ld_w, ld_xr]
            for kc in range(NCH):
                if mo == 0:
                    w = (w0 if kc == 0 else []) + [w_norm[kc]]
                else:
                    w = (w0 + [w_norm[NCH - 1]]) if kc == 0 else []
                S.op("tensor", "matmul", pb[bk][:, 0:512],
                     wsl[:, kc * 128:(kc + 1) * 128],
                     yraw[:, kc * 512:(kc + 1) * 512],
                     start=(kc == 0), stop=False, waits=w)
            S.op("tensor", "matmul", pb[bk][:, 0:512],
                 bprojb[0:1, mo * 128:(mo + 1) * 128], ones512,
                 start=False, stop=True)
            w_mm = S.ms("tensor")
            w_wp_used[gm2] = w_mm
            S.op("vector", "tensor_add", xrt[:, mo * 512:(mo + 1) * 512],
                 xrt[:, mo * 512:(mo + 1) * 512], pb[bk][:, 0:512], waits=[w_mm])
            w_ores[mo] = S.ms("vector")
            bgs(c, bk, w_ores[mo])
        st_x = S.dma("sync", tile_view(c.xres, qlo, MEGA), xrt,
                     waits=[w_ores[NCH - 1], w_ores[NCH - 2]], ch="cst")
        for cc in range(NCH):
            xsl2 = xb_t[:, (cc % 2) * 512:(cc % 2) * 512 + 512]
            x2sl = x2b[:, (cc % 2) * 512:(cc % 2) * 512 + 512]
            w = [w_ores[cc]] + ([w_stmm[cc - 2]] if cc >= 2 else [])
            if cc == 0:
                w.append(w_wp_used[qm * NCH + NCH - 1])
            S.op("vector", "tensor_copy", xsl2, xrt[:, cc * 512:(cc + 1) * 512],
                 waits=w)
            S.chain("vector")
            S.op("vector", "tensor_mul", x2sl, xsl2, xsl2)
            wp2 = S.ms("vector")
            w = bgw(c, "A") if cc == 0 else []
            S.op("tensor", "matmul", pb["A"][0:1, 0:512], c.onescb, xsl2,
                 start=(cc == 0), stop=(cc == 7), waits=[wp2] + w)
            w2 = bgw(c, "B") if cc == 0 else []
            S.op("tensor", "matmul", pb["B"][0:1, 0:512], c.onescb, x2sl,
                 start=(cc == 0), stop=(cc == 7), waits=w2)
            w_stmm[cc] = S.ms("tensor")
        w_st = w_stmm[NCH - 1]
        w = [w_st, st_x] + ([w_r12C] if qm > 0 else [])
        S.op("vector", "tensor_copy", c.r1, pb["A"][0:1, 0:512], waits=w)
        bgs(c, "A", S.ms("vector"))
        S.op("vector", "tensor_copy", c.r2, pb["B"][0:1, 0:512])
        w_xrt2 = S.ms("vector")
        bgs(c, "B", w_xrt2)
        w_q_used[qm] = w_lastS
        w_r12C = S.dma("sync", c.stats_h[0][None, qlo:qlo + MEGA], c.r1,
                       waits=[w_xrt2], ch="css")
        w_r12C = S.dma("sync", c.stats_h[1][None, qlo:qlo + MEGA], c.r2, ch="css")
    stats_to_mrs(c, w_r12C)
    if DBG_DUMP:
        dst4 = nc.dram_tensor("dbg_stats", [4, T], F32, kind="ExternalOutput").ap()
        S.dma("sync", dst4, c.stats_h, waits=[c.w_lnrows], ch="dbg")
        S.barrier()
    stC.close()


def emit_out(c, src="xres"):
    S, nc = c.S, c.nc
    pb = c.pbank
    with nc.sbuf_tensor("otm", [128, 2 * DM], F32) as otg, \
         nc.sbuf_tensor("ocm", [128, 2 * DM], F32) as ocg:
        otm, ocm = otg.ap(), ocg.ap()
        w_out = {}
        w_ev = {}
        for j in range(32):
            b = j % 2
            oc = ocm[:, b * DM:(b + 1) * DM]
            ot = otm[:, b * DM:(b + 1) * DM]
            w = [w_ev[j - 2]] if j >= 2 else []
            ld = S.dma("sync", oc,
                       c.xres.rearrange("c p t -> p c t")[:, :, j * 128:(j + 1) * 128],
                       waits=w, ch="eo%d" % b)
            for cc in range(NCH):
                bk = "A" if cc % 2 == 0 else "B"
                w2 = bgw(c, bk) + ([ld] if cc < 2 else [])
                S.op("tensor", "transpose", pb[bk][:, 0:128],
                     oc[:, cc * 128:(cc + 1) * 128], c.idf, waits=w2)
                wt = S.ms("tensor")
                w3 = [wt] + ([w_out[j - 2]] if (j >= 2 and cc == 0) else [])
                S.op("scalar", "copy", ot[:, cc * 128:(cc + 1) * 128], pb[bk][:, 0:128],
                     waits=w3)
                bgs(c, bk, S.ms("scalar"))
            w_ev[j] = S.now("tensor")
            w_sc = S.now("scalar")
            w_out[j] = S.dma("sync", c.y_out[j * 128:(j + 1) * 128, :], ot, waits=[w_sc], ch="es%d" % b)


def host_inputs(xb, W):
    d = {}
    d["x"] = np.ascontiguousarray(xb)
    win = W["L0_attn_a_in_proj_w"]
    win_p = np.zeros((NMCH, DM, 128), np.float32)
    for m in range(NMCH):
        cols = win[:, m * 128:min((m + 1) * 128, 4384)]
        win_p[m, :, :cols.shape[1]] = cols
    d["win_l"] = win_p
    d["wout"] = np.ascontiguousarray(W["L0_attn_a_out_proj_w"])
    cw = W["L0_attn_a_conv_w"]
    cwl = np.zeros((128, NXBC * 4), np.float32)
    for cc in range(NXBC):
        for k in range(4):
            cwl[:, cc * 4 + k] = cw[k, cc * 128:(cc + 1) * 128]
    d["conv_w"] = cwl
    d["conv_b"] = np.ascontiguousarray(W["L0_attn_a_conv_b"].reshape(NXBC, 128).T)
    d["norm_w"] = np.ascontiguousarray(W["L0_attn_a_norm_w"].reshape(16, 128).T)
    d["d_rep"] = np.ascontiguousarray(np.repeat(W["L0_attn_a_D"], HP).reshape(16, 128).T)
    d["dt_bias"] = np.ascontiguousarray(W["L0_attn_a_dt_bias"].reshape(32, 1))
    d["a_log"] = np.ascontiguousarray(W["L0_attn_a_A_log"].reshape(32, 1))
    wq = W["L0_attn_b_w_attn"]
    d["wqkv_l"] = np.ascontiguousarray(wq.reshape(DM, 24, 128).transpose(1, 0, 2))
    d["b_attn"] = np.ascontiguousarray(W["L0_attn_b_b_attn"].reshape(24, 128).T)
    d["wproj"] = np.ascontiguousarray(W["L0_attn_b_w_proj"])
    d["b_proj"] = np.ascontiguousarray(W["L0_attn_b_b_proj"].reshape(NCH, 128).T)
    d["wrT"] = np.ascontiguousarray(W["L0_ffn_W_r"].T)
    d["wkT"] = np.ascontiguousarray(W["L0_ffn_W_k"].T)
    d["wvT"] = np.ascontiguousarray(W["L0_ffn_W_v"].T)
    d["mu_r"] = np.ascontiguousarray(W["L0_ffn_mu_r"].reshape(NCH, 128).T)
    d["mu_k"] = np.ascontiguousarray(W["L0_ffn_mu_k"].reshape(NCH, 128).T)
    d["c_id"] = np.eye(128, dtype=np.float32)
    u = np.arange(128)
    d["c_tril"] = (u[:, None] <= u[None, :]).astype(np.float32)
    d["c_trius"] = (u[:, None] > u[None, :]).astype(np.float32)
    d["c_mask01"] = (u[None, :] >= u[:, None]).astype(np.float32)
    inv = 1.0 / 10000.0 ** (np.arange(0, HP, 2, dtype=np.float32) / HP)
    fr = np.arange(T, dtype=np.float32)[:, None] * inv[None, :]
    emb = np.repeat(fr, 2, axis=-1)
    d["c_cos"] = np.ascontiguousarray(np.tile(np.cos(emb).T, (2, 1)))
    d["c_sin"] = np.ascontiguousarray(np.tile(np.sin(emb).T, (2, 1)))
    perm = np.zeros((128, 128), np.float32)
    for blk in range(2):
        o = blk * 64
        for i in range(32):
            perm[o + 2 * i + 1, o + 2 * i] = -1.0
            perm[o + 2 * i, o + 2 * i + 1] = 1.0
    d["c_perm"] = perm
    d["c_ones"] = np.ones((128, 128), np.float32)
    cb = np.zeros((128, 2), np.float32)
    cb[:, 0] = EPS
    cb[:, 1] = -SHIFT
    d["c_bias"] = cb
    d["c_onec"] = np.ones((128, 1), np.float32)
    d["battn_r"] = np.ascontiguousarray(W["L0_attn_b_b_attn"].reshape(1, 3072))
    d["bproj_r"] = np.ascontiguousarray(W["L0_attn_b_b_proj"].reshape(1, 1024))
    return d


_prog_cache = {}


def kernel(**inputs):
    key = (DBG_PHASES, DBG_DUMP)
    if key not in _prog_cache:
        _prog_cache[key] = build_core_program()
    nc = _prog_cache[key]
    x = inputs["L0_attn_a_x"]
    in_maps = [host_inputs(x[b], inputs) for b in range(8)]
    res = run_bass_kernel_spmd(nc, in_maps, core_ids=list(range(8)))
    return np.stack([res.results[b]["y"] for b in range(8)], axis=0)
